# version 01
# baseline (speedup 1.0000x reference)
import math
import numpy as np
import concourse.bass as bass
import concourse.mybir as mybir
from concourse.bass_utils import run_bass_kernel_spmd
from contextlib import ExitStack

F32 = mybir.dt.float32
BF16 = mybir.dt.bfloat16
AF = mybir.ActivationFunctionType
ALU = mybir.AluOpType
AX = mybir.AxisListType.X

NB = 4
L = 2048
D = 1024
DI = 2048
DS = 16
NE = 8
EDIM = 2048
TC = 512
NTB = L // TC
KC = D // 128
NDC = DI // 128
EC = EDIM // 128
NTT = TC // 128

GK1 = 2.0 * 0.7978845608028654
GK2 = GK1 * 0.044715
SQRT_GK2 = math.sqrt(GK2)

ENGINES = ("sync", "vector", "scalar", "tensor", "gpsimd")


class Sem:
    def __init__(self, stack, nc, name):
        self.h = stack.enter_context(nc.semaphore(name))
        self.n = 0


class Prog:
    def __init__(self):
        self.streams = {e: [] for e in ENGINES}

    def op(self, eng, fn, sem, k=1):
        sem.n += k

        def wrapped(e, fn=fn, sem=sem, k=k):
            fn(e).then_inc(sem.h, k)
        self.streams[eng].append(wrapped)
        return sem.n

    def wait(self, eng, sem, val=None):
        v = sem.n if val is None else val
        if v <= 0:
            return
        self.streams[eng].append(lambda e, s=sem, v=v: e.wait_ge(s.h, v))

    def barrier(self, sems):
        for eng in ENGINES:
            for s in sems:
                self.wait(eng, s)


def build_program(dbg=False, phases="ABC"):
    import os
    a_stage = os.environ.get("A_STAGE", "dt")
    a_ntb = int(os.environ.get("A_NTB", NTB))
    a_dt_m = int(os.environ.get("A_DT_M", NDC))
    a_dt_sub = int(os.environ.get("A_DT_SUB", 7))
    nc = bass.Bass("TRN2", target_bir_lowering=False)

    x_in = nc.dram_tensor("x", [NB, L, D], F32, kind="ExternalInput").ap()
    out = nc.dram_tensor("out", [NB, L, D], F32, kind="ExternalOutput").ap()

    w = {}
    for name, shape in [
        ("norm1_w", [D]), ("norm2_w", [D]), ("in_proj_w", [D, 2 * DI]),
        ("conv_w", [DI, 4]), ("conv_b", [DI]), ("x_proj_w", [DI, 96]),
        ("dt_proj_w", [64, DI]), ("dt_proj_b", [DI]), ("A_log", [DI, DS]),
        ("D_param", [DI]), ("out_proj_w", [DI, D]), ("router_w", [D, NE]),
        ("w1", [NE, D, EDIM]), ("b1", [NE, EDIM]), ("w2", [NE, EDIM, D]),
        ("b2", [NE, D]), ("gate_w", [D, NE]), ("ffn_w0", [D, D]),
        ("ffn_b0", [D]), ("ffn_w1", [D, D]), ("ffn_b1", [D]),
        ("ident_bf", [128, 128]), ("ident_f32", [128, 128]),
    ]:
        w[name] = nc.dram_tensor(name, shape, BF16 if name == "ident_bf" else F32,
                                 kind="ExternalInput").ap()

    skind = "ExternalOutput" if dbg else "Internal"
    s_xc = nc.dram_tensor("s_xc", [NB, NDC, 128, L], F32, kind=skind).ap()
    s_z = nc.dram_tensor("s_z", [NB, NDC, 128, L], F32, kind=skind).ap()
    s_dt = nc.dram_tensor("s_dt", [NB, NDC, 128, L], F32, kind=skind).ap()
    s_bc = nc.dram_tensor("s_bc", [NB, 2 * DS, L], F32, kind=skind).ap()
    s_mix = nc.dram_tensor("s_mix", [NB, KC, 128, L], F32, kind=skind).ap()
    s_gates = nc.dram_tensor("s_gates", [NB, NTB, NE, TC], F32, kind=skind).ap()
    s_coef = nc.dram_tensor("s_coef", [NB, NTB, 2, TC], F32, kind=skind).ap()
    s_ipbf = nc.dram_tensor("s_ipbf", [D, 2 * DI], BF16, kind="Internal").ap()
    s_opbf = nc.dram_tensor("s_opbf", [DI, D], BF16, kind="Internal").ap()
    s_w1bf = nc.dram_tensor("s_w1bf", [NE, D, EDIM], BF16, kind="Internal").ap()
    s_w2bf = nc.dram_tensor("s_w2bf", [NE, EDIM, D], BF16, kind="Internal").ap()
    s_w0t = nc.dram_tensor("s_w0t", [KC, 128, D], BF16, kind="Internal").ap()
    s_w1t = nc.dram_tensor("s_w1t", [KC, 128, D], BF16, kind="Internal").ap()

    P = Prog()

    with ExitStack() as ctx:
        E = ctx.enter_context
        lset = Sem(ctx, nc, "lset")
        gd = Sem(ctx, nc, "gd")
        lx = Sem(ctx, nc, "lx")
        lw = Sem(ctx, nc, "lw")
        lact = Sem(ctx, nc, "lact")
        lbc = Sem(ctx, nc, "lbc")
        lmix = Sem(ctx, nc, "lmix")
        lwt = Sem(ctx, nc, "lwt")
        sta = Sem(ctx, nc, "sta")
        stb = Sem(ctx, nc, "stb")
        stc = Sem(ctx, nc, "stc")
        pe = Sem(ctx, nc, "pe")
        ve = Sem(ctx, nc, "ve")
        se = Sem(ctx, nc, "se")
        ALLSEMS = [lset, gd, lx, lw, lact, lbc, lmix, lwt, sta, stb, stc, pe, ve, se]

        def dma(dst, src, sem, q="sync", slow=False):
            return P.op(q, lambda e, d=dst, s=src, sl=slow: e.dma_start(
                out=d, in_=s, allow_slow_non_contiguous=sl), sem, 16)

        def store(dst, src, sem):
            return P.op("sync", lambda e, d=dst, s=src: e.dma_start(out=d, in_=s), sem, 16)

        ones_col_bf = E(nc.sbuf_tensor("ones_col_bf", [128, 1], BF16))
        ones_row_f = E(nc.sbuf_tensor("ones_row_f", [1, 128], F32))
        id_bf = E(nc.sbuf_tensor("id_bf", [128, 128], BF16))
        id_f = E(nc.sbuf_tensor("id_f", [128, 128], F32))
        n1w = E(nc.sbuf_tensor("n1w", [128, KC], F32))
        n2w = E(nc.sbuf_tensor("n2w", [128, KC], F32))
        convw = E(nc.sbuf_tensor("convw", [128, NDC * 4], F32))
        convb = E(nc.sbuf_tensor("convb", [128, NDC], F32))
        dtb = E(nc.sbuf_tensor("dtb", [128, NDC], F32))
        A_sb = E(nc.sbuf_tensor("A_sb", [128, NDC * DS], F32))
        D_sb = E(nc.sbuf_tensor("D_sb", [128, NDC], F32))
        b1_sb = E(nc.sbuf_tensor("b1_sb", [128, NE * EC], F32))
        b2_bf = E(nc.sbuf_tensor("b2_bf", [NE, D], BF16))
        fb0 = E(nc.sbuf_tensor("fb0", [128, KC], F32))
        fb1 = E(nc.sbuf_tensor("fb1", [128, KC], F32))
        xpw_bf = E(nc.sbuf_tensor("xpw_bf", [128, NDC * 96], BF16))
        dtw_bf = E(nc.sbuf_tensor("dtw_bf", [64, DI], BF16))
        rww_bf = E(nc.sbuf_tensor("rww_bf", [128, KC * NE], BF16))
        gww_bf = E(nc.sbuf_tensor("gww_bf", [128, KC * NE], BF16))
        state_sb = E(nc.sbuf_tensor("state_sb", [128, NDC * DS], F32))
        hist_bf = E(nc.sbuf_tensor("hist_bf", [128, NDC * 3], BF16))

        const_sb = E(nc.sbuf_tensor("const_sb", [128, 4], F32))
        for col, val in ((0, 1e-6), (1, 1e-5), (2, 1.0)):
            P.op("vector", lambda e, col=col, val=val: e.memset(const_sb[:, col:col + 1], val), ve)
            nc.const_aps.register(F32, val, const_sb[:, col:col + 1])

        dma(n1w[:], w["norm1_w"].rearrange("(c p) -> p c", p=128), lset, slow=True)
        dma(n2w[:], w["norm2_w"].rearrange("(c p) -> p c", p=128), lset, slow=True)
        dma(convw[:].rearrange("p (c k) -> p c k", k=4), w["conv_w"].rearrange("(c p) k -> p c k", p=128), lset)
        dma(convb[:], w["conv_b"].rearrange("(c p) -> p c", p=128), lset, slow=True)
        dma(dtb[:], w["dt_proj_b"].rearrange("(c p) -> p c", p=128), lset, slow=True)
        dma(D_sb[:], w["D_param"].rearrange("(c p) -> p c", p=128), lset, slow=True)
        dma(b1_sb[:].rearrange("p (e c) -> p e c", e=NE), w["b1"].rearrange("e (c p) -> p e c", p=128), lset, slow=True)
        dma(fb0[:], w["ffn_b0"].rearrange("(c p) -> p c", p=128), lset, slow=True)
        dma(fb1[:], w["ffn_b1"].rearrange("(c p) -> p c", p=128), lset, slow=True)
        dma(id_bf[:], w["ident_bf"], lset)
        dma(id_f[:], w["ident_f32"], lset)
        dma(A_sb[:].rearrange("p (c s) -> p c s", s=DS), w["A_log"].rearrange("(c p) s -> p c s", p=128), lset)
        dma(b2_bf[:], w["b2"], gd, q="gpsimd")
        dma(xpw_bf[:].rearrange("p (c m) -> p c m", m=96), w["x_proj_w"].rearrange("(c p) m -> p c m", p=128), gd, q="gpsimd")
        dma(dtw_bf[:], w["dt_proj_w"], gd, q="gpsimd")
        dma(rww_bf[:].rearrange("p (c m) -> p c m", m=NE), w["router_w"].rearrange("(c p) m -> p c m", p=128), gd, q="gpsimd")
        dma(gww_bf[:].rearrange("p (c m) -> p c m", m=NE), w["gate_w"].rearrange("(c p) m -> p c m", p=128), gd, q="gpsimd")
        dma(s_ipbf, w["in_proj_w"], gd, q="gpsimd")
        dma(s_opbf, w["out_proj_w"], gd, q="gpsimd")
        for e_i in range(NE):
            dma(s_w1bf[e_i], w["w1"][e_i], gd, q="gpsimd")
            dma(s_w2bf[e_i], w["w2"][e_i], gd, q="gpsimd")
        P.wait("scalar", lset)
        P.op("scalar", lambda e: e.activation(out=A_sb[:], in_=A_sb[:], func=AF.Exp), se)
        P.wait("vector", se)
        P.op("vector", lambda e: e.tensor_scalar_mul(out=A_sb[:], in0=A_sb[:], scalar1=-1.0), ve)
        P.wait("vector", lset)
        P.op("vector", lambda e: e.memset(ones_col_bf[:], 1.0), ve)
        P.op("vector", lambda e: e.memset(ones_row_f[:], 1.0), ve)

        with nc.psum_tensor("ps_t0", [128, 128], BF16) as ps_t0, \
                nc.sbuf_tensor("wtmp", [128, D], BF16) as wtmp, \
                nc.sbuf_tensor("wttmp", [128, D], BF16) as wttmp:
            for wsrc, sdst in ((w["ffn_w0"], s_w0t), (w["ffn_w1"], s_w1t)):
                for mc in range(KC):
                    P.wait("gpsimd", sta)
                    dma(wtmp[:], wsrc[mc * 128:(mc + 1) * 128, :], lwt, q="gpsimd")
                    for kc2 in range(KC):
                        P.wait("tensor", lwt)
                        P.wait("tensor", ve)
                        P.op("tensor", lambda e, kc2=kc2: e.transpose(
                            out=ps_t0[:], in_=wtmp[:, kc2 * 128:(kc2 + 1) * 128],
                            identity=id_bf[:]), pe)
                        P.wait("vector", pe)
                        P.op("vector", lambda e, kc2=kc2: e.tensor_copy(
                            out=wttmp[:, kc2 * 128:(kc2 + 1) * 128], in_=ps_t0[:]), ve)
                    P.wait("sync", ve)
                    for kc2 in range(KC):
                        store(sdst[kc2, :, mc * 128:(mc + 1) * 128],
                              wttmp[:, kc2 * 128:(kc2 + 1) * 128], sta)
        P.barrier(ALLSEMS)

        def emit_x_load(b, tb, x_fm, xtm, ps_tr):
            t0 = tb * TC
            P.wait("sync", stc)
            for tt in range(NTT):
                P.wait("sync", pe)
                dma(xtm[tt % 2][:], x_in[b, t0 + tt * 128: t0 + (tt + 1) * 128, :], lx)
                for kc in range(KC):
                    P.wait("tensor", lx)
                    P.wait("tensor", ve)
                    P.op("tensor", lambda e, tt=tt, kc=kc: e.transpose(
                        out=ps_tr[0][:], in_=xtm[tt % 2][:, kc * 128:(kc + 1) * 128],
                        identity=id_f[:]), pe)
                    P.wait("vector", pe)
                    P.op("vector", lambda e, tt=tt, kc=kc: e.tensor_copy(
                        out=x_fm[:, kc * TC + tt * 128: kc * TC + (tt + 1) * 128],
                        in_=ps_tr[0][:]), ve)

        def emit_rmsnorm(x_fm, wsb, out_fn, sq_bf, ps_stat, ps_bcst, rrow, eps=1e-6):
            P.wait("scalar", ve)
            P.op("scalar", lambda e: e.activation(out=sq_bf[:], in_=x_fm[:], func=AF.Square), se)
            P.wait("tensor", se)
            P.wait("tensor", ve)
            for kc in range(KC):
                P.op("tensor", lambda e, kc=kc: e.matmul(
                    out=ps_stat[0:1, 0:TC], lhsT=ones_col_bf[:],
                    rhs=sq_bf[:, kc * TC:(kc + 1) * TC],
                    start=(kc == 0), stop=(kc == KC - 1)), pe)
            P.wait("scalar", pe)
            P.op("scalar", lambda e: e.activation(out=rrow[0:1, 0:TC], in_=ps_stat[0:1, 0:TC],
                                                  func=AF.Ln, scale=1.0 / D, bias=eps), se)
            P.op("scalar", lambda e: e.activation(out=rrow[0:1, 0:TC], in_=rrow[0:1, 0:TC],
                                                  func=AF.Exp, scale=-0.5), se)
            P.wait("tensor", se)
            P.op("tensor", lambda e: e.matmul(out=ps_bcst[:, 0:TC], lhsT=ones_row_f[:],
                                              rhs=rrow[0:1, 0:TC], start=True, stop=True), pe)
            P.wait("vector", pe)
            for kc in range(KC):
                P.op("vector", lambda e, kc=kc: e.scalar_tensor_tensor(
                    out=out_fn(kc), in0=x_fm[:, kc * TC:(kc + 1) * TC],
                    scalar=wsb[:, kc:kc + 1], in1=ps_bcst[:, 0:TC],
                    op0=ALU.mult, op1=ALU.mult), ve)

        def emit_layernorm(xin, out_f, cast_bf, sq_bf, ps_stat, ps_bc, rows, trow, eps=1e-5):
            P.wait("vector", se)
            P.op("vector", lambda e: e.tensor_copy(out=cast_bf[:], in_=xin[:]), ve)
            P.wait("scalar", ve)
            P.op("scalar", lambda e: e.activation(out=sq_bf[:], in_=xin[:], func=AF.Square), se)
            P.wait("tensor", se)
            P.wait("tensor", ve)
            for kc in range(KC):
                P.op("tensor", lambda e, kc=kc: e.matmul(
                    out=ps_stat[0:1, 0:TC], lhsT=ones_col_bf[:],
                    rhs=cast_bf[:, kc * TC:(kc + 1) * TC],
                    start=(kc == 0), stop=(kc == KC - 1)), pe)
            for kc in range(KC):
                P.op("tensor", lambda e, kc=kc: e.matmul(
                    out=ps_stat[0:1, TC:2 * TC], lhsT=ones_col_bf[:],
                    rhs=sq_bf[:, kc * TC:(kc + 1) * TC],
                    start=(kc == 0), stop=(kc == KC - 1)), pe)
            P.wait("scalar", pe)
            P.op("scalar", lambda e: e.activation(out=rows[0:1, 0:TC], in_=ps_stat[0:1, 0:TC],
                                                  func=AF.Copy, scale=1.0 / D), se)
            P.op("scalar", lambda e: e.activation(out=rows[0:1, TC:2 * TC],
                                                  in_=ps_stat[0:1, TC:2 * TC],
                                                  func=AF.Copy, scale=1.0 / D), se)
            P.wait("vector", se)
            P.op("vector", lambda e: e.tensor_mul(out=trow[0:1, 0:TC], in0=rows[0:1, 0:TC],
                                                  in1=rows[0:1, 0:TC]), ve)
            P.op("vector", lambda e: e.tensor_sub(out=trow[0:1, TC:2 * TC],
                                                  in0=rows[0:1, TC:2 * TC],
                                                  in1=trow[0:1, 0:TC]), ve)
            P.wait("scalar", ve)
            P.op("scalar", lambda e: e.activation(out=trow[0:1, TC:2 * TC],
                                                  in_=trow[0:1, TC:2 * TC],
                                                  func=AF.Ln, bias=eps), se)
            P.op("scalar", lambda e: e.activation(out=trow[0:1, TC:2 * TC],
                                                  in_=trow[0:1, TC:2 * TC],
                                                  func=AF.Exp, scale=-0.5), se)
            P.wait("vector", se)
            P.op("vector", lambda e: e.tensor_mul(out=rows[0:1, TC:2 * TC],
                                                  in0=rows[0:1, 0:TC],
                                                  in1=trow[0:1, TC:2 * TC]), ve)
            P.op("vector", lambda e: e.tensor_copy(out=rows[0:1, 0:TC],
                                                   in_=trow[0:1, TC:2 * TC]), ve)
            P.wait("tensor", ve)
            P.op("tensor", lambda e: e.matmul(out=ps_bc[0][:, 0:TC], lhsT=ones_row_f[:],
                                              rhs=rows[0:1, 0:TC], start=True, stop=True), pe)
            P.op("tensor", lambda e: e.matmul(out=ps_bc[1][:, 0:TC], lhsT=ones_row_f[:],
                                              rhs=rows[0:1, TC:2 * TC], start=True, stop=True), pe)
            P.wait("vector", pe)
            for kc in range(KC):
                P.op("vector", lambda e, kc=kc: e.tensor_mul(
                    out=out_f[:, kc * TC:(kc + 1) * TC], in0=xin[:, kc * TC:(kc + 1) * TC],
                    in1=ps_bc[0][:, 0:TC]), ve)
            for kc in range(KC):
                P.op("vector", lambda e, kc=kc: e.tensor_sub(
                    out=out_f[:, kc * TC:(kc + 1) * TC], in0=out_f[:, kc * TC:(kc + 1) * TC],
                    in1=ps_bc[1][:, 0:TC]), ve)

        def emit_topk_gates(lgtm, gtm, tk, se_ref):
            col = lambda tt, i: tk[:, tt * 20 + i: tt * 20 + i + 1]
            lgm_ = lambda tt: tk[:, tt * 20 + 9: tt * 20 + 17]
            lg8_ = lambda tt: lgtm[:, tt * 8:(tt + 1) * 8]
            g8_ = lambda tt: gtm[:, tt * 8:(tt + 1) * 8]
            for tt in range(NTT):
                P.op("vector", lambda e, tt=tt: e.reduce_max(
                    out=col(tt, 0), in_=lg8_(tt), axis=AX), ve)
            for tt in range(NTT):
                P.op("vector", lambda e, tt=tt: e.tensor_scalar(
                    out=g8_(tt), in0=lg8_(tt), scalar1=col(tt, 0), scalar2=None,
                    op0=ALU.is_equal), ve)
            for tt in range(NTT):
                P.op("vector", lambda e, tt=tt: e.scalar_tensor_tensor(
                    out=lgm_(tt), in0=g8_(tt), scalar=-1e30, in1=lg8_(tt),
                    op0=ALU.mult, op1=ALU.add), ve)
            for tt in range(NTT):
                P.op("vector", lambda e, tt=tt: e.reduce_max(
                    out=col(tt, 1), in_=lgm_(tt), axis=AX), ve)
            for tt in range(NTT):
                dv = P.op("vector", lambda e, tt=tt: e.tensor_sub(
                    out=col(tt, 2), in0=col(tt, 1), in1=col(tt, 0)), ve)
            P.wait("scalar", ve, dv)
            for tt in range(NTT):
                sv = P.op("scalar", lambda e, tt=tt: e.activation(
                    out=col(tt, 3), in_=col(tt, 2), func=AF.Sigmoid, scale=-1.0), se)
            P.wait("vector", se, sv)
            for tt in range(NTT):
                P.op("vector", lambda e, tt=tt: e.tensor_scalar(
                    out=col(tt, 4), in0=col(tt, 3), scalar1=-1.0, scalar2=1.0,
                    op0=ALU.mult, op1=ALU.add), ve)
            for tt in range(NTT):
                P.op("vector", lambda e, tt=tt: e.tensor_scalar_mul(
                    out=g8_(tt), in0=g8_(tt), scalar1=col(tt, 3)), ve)
            for tt in range(NTT):
                P.op("vector", lambda e, tt=tt: e.tensor_scalar(
                    out=lgm_(tt), in0=lgm_(tt), scalar1=col(tt, 1), scalar2=None,
                    op0=ALU.is_equal), ve)
            for tt in range(NTT):
                P.op("vector", lambda e, tt=tt: e.scalar_tensor_tensor(
                    out=g8_(tt), in0=lgm_(tt), scalar=col(tt, 4), in1=g8_(tt),
                    op0=ALU.mult, op1=ALU.add), ve)

        def emit_logit_transpose(lg_sb, lgtm, ps_tr):
            P.wait("tensor", ve)
            for tt in range(NTT):
                P.op("tensor", lambda e, tt=tt: e.transpose(
                    out=ps_tr[0][0:128, 0:8], in_=lg_sb[:, tt * 128:(tt + 1) * 128],
                    identity=id_f[0:8, 0:8]), pe)
                P.wait("vector", pe)
                P.op("vector", lambda e, tt=tt: e.tensor_copy(
                    out=lgtm[:, tt * 8:(tt + 1) * 8], in_=ps_tr[0][0:128, 0:8]), ve)
                P.wait("tensor", ve)

        def emit_rows_back(gtm, gfm, nrows, ps_tr):
            P.wait("tensor", ve)
            for tt in range(NTT):
                P.op("tensor", lambda e, tt=tt: e.transpose(
                    out=ps_tr[0][0:8, 0:128], in_=gtm[:, tt * 8:(tt + 1) * 8],
                    identity=id_f[:]), pe)
                P.wait("vector", pe)
                P.op("vector", lambda e, tt=tt, nrows=nrows: e.tensor_copy(
                    out=gfm[0:nrows, tt * 128:(tt + 1) * 128],
                    in_=ps_tr[0][0:nrows, 0:128]), ve)
                P.wait("tensor", ve)

        def phase_A(b):
            with ExitStack() as actx:
                AE = actx.enter_context
                inproj_sb = AE(nc.sbuf_tensor(f"inproj_sb_{b}", [128, KC * 4096], BF16))
                diag_all = AE(nc.sbuf_tensor(f"diag_all_{b}", [128, NDC * 4 * 128], BF16))
                x_fm = AE(nc.sbuf_tensor(f"xa_fm_{b}", [128, KC * TC], F32))
                xtm = [AE(nc.sbuf_tensor(f"xtm{i}_{b}", [128, D], F32)) for i in range(2)]
                sq_bf = AE(nc.sbuf_tensor(f"sqa_bf_{b}", [128, KC * TC], BF16))
                h_bf = AE(nc.sbuf_tensor(f"h_bf_{b}", [128, KC * TC], BF16))
                rrow = AE(nc.sbuf_tensor(f"rrow_a_{b}", [1, 2 * TC], F32))
                xcraw = AE(nc.sbuf_tensor(f"xcraw_{b}", [128, NDC * TC], BF16))
                xcs_bf = AE(nc.sbuf_tensor(f"xcs_bf_{b}", [128, NDC * TC], BF16))
                ztile = [AE(nc.sbuf_tensor(f"ztile{i}_{b}", [128, TC], F32)) for i in range(2)]
                xtile = [AE(nc.sbuf_tensor(f"xtile{i}_{b}", [128, TC], F32)) for i in range(2)]
                dtile = [AE(nc.sbuf_tensor(f"dtile{i}_{b}", [128, TC], F32)) for i in range(2)]
                proj_sb = AE(nc.sbuf_tensor(f"proj_sb_{b}", [96, TC], F32))
                dtr_bf = AE(nc.sbuf_tensor(f"dtr_bf_{b}", [64, TC], BF16))
                sp1 = AE(nc.sbuf_tensor(f"sp1_{b}", [128, TC], F32))
                sp2 = AE(nc.sbuf_tensor(f"sp2_{b}", [128, TC], F32))
                spr = AE(nc.sbuf_tensor(f"spr_{b}", [128, TC], F32))
                ps_mm = [AE(nc.psum_tensor(f"ps_mma{i}_{b}", [128, TC], F32)) for i in range(2)]
                ps_cv = AE(nc.psum_tensor(f"ps_cv_{b}", [128, TC], F32))
                ps_xp = AE(nc.psum_tensor(f"ps_xp_{b}", [96, TC], F32))
                ps_st = AE(nc.psum_tensor(f"ps_sta_{b}", [128, 2 * TC], F32))
                ps_tr = [AE(nc.psum_tensor(f"ps_tra_{b}", [128, 128], F32))]

                for kc in range(KC):
                    dma(inproj_sb[:, kc * 4096:(kc + 1) * 4096],
                        s_ipbf[kc * 128:(kc + 1) * 128, :], lw)
                P.wait("vector", lset)
                for dc in range(NDC):
                    for k in range(4):
                        P.op("vector", lambda e, dc=dc, k=k: e.tensor_scalar_mul(
                            out=diag_all[:, (dc * 4 + k) * 128:(dc * 4 + k + 1) * 128],
                            in0=id_bf[:], scalar1=convw[:, dc * 4 + k: dc * 4 + k + 1]), ve)
                P.op("vector", lambda e: e.memset(hist_bf[:], 0.0), ve)

                for tb in range(a_ntb):
                    t0 = tb * TC
                    emit_x_load(b, tb, x_fm, xtm, ps_tr)
                    if a_stage == "x":
                        continue
                    emit_rmsnorm(x_fm, n1w,
                                 lambda kc: h_bf[:, kc * TC:(kc + 1) * TC],
                                 sq_bf, ps_st, ps_mm[0], rrow)
                    if a_stage == "norm":
                        continue
                    P.wait("tensor", lw)
                    for m in range(32):
                        pb = ps_mm[m % 2]
                        if m < 2:
                            P.wait("tensor", ve)
                        else:
                            P.wait("tensor", se)
                            P.wait("tensor", ve)
                        for kc in range(KC):
                            P.op("tensor", lambda e, m=m, kc=kc, pb=pb: e.matmul(
                                out=pb[:, 0:TC],
                                lhsT=inproj_sb[:, kc * 4096 + m * 128: kc * 4096 + (m + 1) * 128],
                                rhs=h_bf[:, kc * TC:(kc + 1) * TC],
                                start=(kc == 0), stop=(kc == KC - 1)), pe)
                        P.wait("scalar", pe)
                        if m < 16:
                            P.op("scalar", lambda e, m=m, pb=pb: e.activation(
                                out=xcraw[:, m * TC:(m + 1) * TC], in_=pb[:, 0:TC],
                                func=AF.Copy), se)
                        else:
                            zt = ztile[m % 2]
                            P.wait("scalar", sta)
                            P.op("scalar", lambda e, pb=pb, zt=zt: e.activation(
                                out=zt[:], in_=pb[:, 0:TC], func=AF.Silu), se)
                            P.wait("sync", se)
                            store(s_z[b, m - 16, :, t0:t0 + TC], zt[:], sta)
                    if a_stage == "ip":
                        continue
                    for dc in range(NDC):
                        P.wait("tensor", se)
                        cur = xcraw[:, dc * TC:(dc + 1) * TC]
                        hist = hist_bf[:, dc * 3:(dc + 1) * 3]
                        dgs = diag_all[:, dc * 4 * 128:(dc + 1) * 4 * 128]
                        if dc >= 1:
                            P.wait("tensor", ve)
                        P.wait("tensor", se)
                        P.op("tensor", lambda e, cur=cur, dgs=dgs: e.matmul(
                            out=ps_cv[:, 0:TC], lhsT=dgs[:, 3 * 128:4 * 128], rhs=cur,
                            start=True, stop=False), pe)
                        for k in range(3):
                            nh = 3 - k
                            P.op("tensor", lambda e, k=k, nh=nh, hist=hist, dgs=dgs: e.matmul(
                                out=ps_cv[:, 0:nh], lhsT=dgs[:, k * 128:(k + 1) * 128],
                                rhs=hist[:, k:3], start=False, stop=False), pe)
                            P.op("tensor", lambda e, k=k, nh=nh, cur=cur, dgs=dgs: e.matmul(
                                out=ps_cv[:, nh:TC], lhsT=dgs[:, k * 128:(k + 1) * 128],
                                rhs=cur[:, 0:TC - nh], start=False, stop=(k == 2)), pe)
                        P.wait("vector", pe)
                        P.op("vector", lambda e, cur=cur, hist=hist: e.tensor_copy(
                            out=hist, in_=cur[:, TC - 3:TC]), ve)
                        P.wait("scalar", pe)
                        P.wait("scalar", sta)
                        xt = xtile[dc % 2]
                        P.op("scalar", lambda e, dc=dc, xt=xt: e.activation(
                            out=xt[:], in_=ps_cv[:, 0:TC], func=AF.Silu,
                            bias=convb[:, dc:dc + 1]), se)
                        P.wait("sync", se)
                        store(s_xc[b, dc, :, t0:t0 + TC], xt[:], sta)
                        P.wait("vector", se)
                        P.op("vector", lambda e, dc=dc, xt=xt: e.tensor_copy(
                            out=xcs_bf[:, dc * TC:(dc + 1) * TC], in_=xt[:]), ve)
                        P.wait("tensor", ve)
                        P.op("tensor", lambda e, dc=dc: e.matmul(
                            out=ps_xp[:, 0:TC], lhsT=xpw_bf[:, dc * 96:(dc + 1) * 96],
                            rhs=xcs_bf[:, dc * TC:(dc + 1) * TC],
                            start=(dc == 0), stop=(dc == NDC - 1)), pe)
                    if a_stage == "conv":
                        continue
                    P.wait("vector", pe)
                    P.op("vector", lambda e: e.tensor_copy(out=proj_sb[:], in_=ps_xp[0:96, 0:TC]), ve)
                    P.op("vector", lambda e: e.tensor_copy(out=dtr_bf[:], in_=proj_sb[0:64, :]), ve)
                    P.wait("sync", ve)
                    store(s_bc[b, :, t0:t0 + TC], proj_sb[64:96, :], sta)
                    P.wait("tensor", ve)
                    for m in range(a_dt_m):
                        pb = ps_mm[m % 2]
                        P.wait("tensor", ve)
                        P.op("tensor", lambda e, m=m, pb=pb: e.matmul(
                            out=pb[:, 0:TC], lhsT=dtw_bf[:, m * 128:(m + 1) * 128],
                            rhs=dtr_bf[:], start=True, stop=True), pe)
                        if a_dt_sub < 1:
                            continue
                        P.wait("scalar", pe)
                        abs_val = P.op("scalar", lambda e, m=m, pb=pb: e.activation(
                            out=sp1[:], in_=pb[:, 0:TC], func=AF.Abs,
                            bias=dtb[:, m:m + 1]), se)
                        if a_dt_sub < 2:
                            continue
                        P.op("scalar", lambda e: e.activation(
                            out=sp2[:], in_=sp1[:], func=AF.Exp, scale=-1.0), se)
                        if a_dt_sub < 3:
                            continue
                        P.op("scalar", lambda e: e.activation(
                            out=sp1[:], in_=sp2[:], func=AF.Ln, bias=1.0), se)
                        if a_dt_sub < 4:
                            continue
                        P.wait("vector", pe)
                        P.wait("vector", se, abs_val)
                        P.op("vector", lambda e, m=m, pb=pb: e.tensor_scalar(
                            out=spr[:], in0=pb[:, 0:TC], scalar1=dtb[:, m:m + 1],
                            scalar2=0.0, op0=ALU.add, op1=ALU.max), ve)
                        if a_dt_sub < 5:
                            continue
                        P.wait("vector", se)
                        P.wait("vector", sta)
                        dt_t = dtile[m % 2]
                        P.op("vector", lambda e, dt_t=dt_t: e.tensor_add(
                            out=dt_t[:], in0=spr[:], in1=sp1[:]), ve)
                        if a_dt_sub < 6:
                            continue
                        P.wait("sync", ve)
                        store(s_dt[b, m, :, t0:t0 + TC], dt_t[:], sta)
                        P.wait("scalar", ve)
                P.barrier(ALLSEMS)

        def phase_B(b):
            with ExitStack() as bctx:
                BE = bctx.enter_context
                cache = BE(nc.sbuf_tensor(f"bc_cache_{b}", [128, 2 * DS * TC], F32))
                dtt = [BE(nc.sbuf_tensor(f"dtt{i}_{b}", [128, TC], F32)) for i in range(2)]
                xct = [BE(nc.sbuf_tensor(f"xct{i}_{b}", [128, TC], F32)) for i in range(2)]
                ztt = [BE(nc.sbuf_tensor(f"ztt{i}_{b}", [128, TC], F32)) for i in range(2)]
                u_t = BE(nc.sbuf_tensor(f"u_t_{b}", [128, TC], F32))
                dA = [BE(nc.sbuf_tensor(f"dA{i}_{b}", [128, TC], F32)) for i in range(3)]
                b_all = BE(nc.sbuf_tensor(f"b_all_{b}", [128, DS * TC], F32))
                h_all = BE(nc.sbuf_tensor(f"h_all_{b}", [128, DS * TC], F32))
                mc_all = BE(nc.sbuf_tensor(f"mc_all_{b}", [128, DS * TC], BF16))
                yt = BE(nc.sbuf_tensor(f"yt_{b}", [128, TC], F32))
                ymix = BE(nc.sbuf_tensor(f"ymix_{b}", [128, NDC * TC], BF16))
                opw = [BE(nc.sbuf_tensor(f"opw{i}_{b}", [128, NDC * 128], BF16)) for i in range(2)]
                mixt = [BE(nc.sbuf_tensor(f"mixt{i}_{b}", [128, TC], F32)) for i in range(2)]
                ps_y = [BE(nc.psum_tensor(f"ps_y{i}_{b}", [128, TC], F32)) for i in range(2)]
                ps_op = [BE(nc.psum_tensor(f"ps_op{i}_{b}", [128, TC], F32)) for i in range(2)]

                for tb in range(NTB):
                    t0 = tb * TC
                    P.wait("sync", ve)
                    for row in range(2 * DS):
                        dma(cache[:, row * TC:(row + 1) * TC],
                            s_bc[b, row:row + 1, t0:t0 + TC].partition_broadcast(128), lbc)
                    P.wait("vector", lbc)
                    tts_ve = {}
                    for dc in range(NDC):
                        io = dc % 2
                        P.wait("sync", ve)
                        P.wait("sync", se)
                        dma(dtt[io][:], s_dt[b, dc, :, t0:t0 + TC], lact)
                        dma(xct[io][:], s_xc[b, dc, :, t0:t0 + TC], lact)
                        dma(ztt[io][:], s_z[b, dc, :, t0:t0 + TC], lact)
                        P.wait("vector", lact)
                        P.op("vector", lambda e, io=io: e.tensor_mul(
                            out=u_t[:], in0=dtt[io][:], in1=xct[io][:]), ve)
                        P.op("vector", lambda e: e.tensor_tensor(
                            out=b_all[:].rearrange("p (s t) -> p s t", s=DS),
                            in0=u_t[:].unsqueeze(1).broadcast_to([128, DS, TC]),
                            in1=cache[:, 0:DS * TC].rearrange("p (s t) -> p s t", s=DS),
                            op=ALU.mult), ve)
                        b_done = ve.n
                        P.wait("scalar", lact)
                        for s in range(DS):
                            da = dA[s % 3]
                            if s >= 3:
                                P.wait("scalar", ve, tts_ve[s - 3])
                            exp_val = P.op("scalar", lambda e, io=io, dc=dc, s=s, da=da: e.activation(
                                out=da[:], in_=dtt[io][:], func=AF.Exp,
                                scale=A_sb[:, dc * DS + s: dc * DS + s + 1]), se)
                            P.wait("vector", se, exp_val)
                            if tb == 0:
                                tts_ve[s] = P.op("vector", lambda e, s=s, da=da: e.tensor_tensor_scan(
                                    out=h_all[:, s * TC:(s + 1) * TC], data0=da[:],
                                    data1=b_all[:, s * TC:(s + 1) * TC],
                                    initial=0.0, op0=ALU.mult, op1=ALU.add), ve)
                            else:
                                tts_ve[s] = P.op("vector", lambda e, dc=dc, s=s, da=da: e.tensor_tensor_scan(
                                    out=h_all[:, s * TC:(s + 1) * TC], data0=da[:],
                                    data1=b_all[:, s * TC:(s + 1) * TC],
                                    initial=state_sb[:, dc * DS + s: dc * DS + s + 1],
                                    op0=ALU.mult, op1=ALU.add), ve)
                        if tb < NTB - 1:
                            P.op("vector", lambda e, dc=dc: e.tensor_copy(
                                out=state_sb[:, dc * DS:(dc + 1) * DS].unsqueeze(2),
                                in_=h_all[:].rearrange("p (s t) -> p s t", s=DS)[:, :, TC - 1:TC]), ve)
                        P.op("vector", lambda e: e.tensor_tensor(
                            out=mc_all[:].rearrange("p (s t) -> p s t", s=DS),
                            in0=h_all[:].rearrange("p (s t) -> p s t", s=DS),
                            in1=cache[:, DS * TC:2 * DS * TC].rearrange("p (s t) -> p s t", s=DS),
                            op=ALU.mult), ve)
                        py = ps_y[dc % 2]
                        P.wait("tensor", ve)
                        for s in range(DS):
                            P.op("tensor", lambda e, s=s, py=py: e.matmul(
                                out=py[:, 0:TC], lhsT=id_bf[:],
                                rhs=mc_all[:, s * TC:(s + 1) * TC],
                                start=(s == 0), stop=(s == DS - 1)), pe)
                        P.wait("vector", pe)
                        P.op("vector", lambda e, io=io, dc=dc, py=py: e.scalar_tensor_tensor(
                            out=yt[:], in0=xct[io][:], scalar=D_sb[:, dc:dc + 1],
                            in1=py[:, 0:TC], op0=ALU.mult, op1=ALU.add), ve)
                        P.op("vector", lambda e, io=io, dc=dc: e.tensor_mul(
                            out=ymix[:, dc * TC:(dc + 1) * TC], in0=yt[:], in1=ztt[io][:]), ve)
                    dma(opw[0][:].rearrange("p (c n) -> p c n", n=128), s_opbf[:, 0:128].rearrange("(c p) n -> p c n", p=128), lw)
                    for m in range(KC):
                        po = ps_op[m % 2]
                        if m + 1 < KC:
                            P.wait("sync", pe)
                            dma(opw[(m + 1) % 2][:].rearrange("p (c n) -> p c n", n=128),
                                s_opbf[:, (m + 1) * 128:(m + 2) * 128].rearrange(
                                    "(c p) n -> p c n", p=128), lw)
                        P.wait("tensor", lw)
                        P.wait("tensor", ve)
                        for dc in range(NDC):
                            P.op("tensor", lambda e, m=m, dc=dc, po=po: e.matmul(
                                out=po[:, 0:TC], lhsT=opw[m % 2][:, dc * 128:(dc + 1) * 128],
                                rhs=ymix[:, dc * TC:(dc + 1) * TC],
                                start=(dc == 0), stop=(dc == NDC - 1)), pe)
                        P.wait("vector", pe)
                        P.wait("vector", stb)
                        mt = mixt[m % 2]
                        P.op("vector", lambda e, po=po, mt=mt: e.tensor_copy(
                            out=mt[:], in_=po[:, 0:TC]), ve)
                        P.wait("sync", ve)
                        store(s_mix[b, m, :, t0:t0 + TC], mt[:], stb)
                P.barrier(ALLSEMS)

        def phase_C(b):
            with ExitStack() as cctx:
                CE = cctx.enter_context
                x_fm = CE(nc.sbuf_tensor(f"xc_fm_{b}", [128, KC * TC], F32))
                xtm = [CE(nc.sbuf_tensor(f"xtmc{i}_{b}", [128, D], F32)) for i in range(2)]
                mix_fm = CE(nc.sbuf_tensor(f"mix_fm_{b}", [128, KC * TC], F32))
                x2_bf = CE(nc.sbuf_tensor(f"x2_bf_{b}", [128, KC * TC], BF16))
                sq_bf = CE(nc.sbuf_tensor(f"sqc_bf_{b}", [128, KC * TC], BF16))
                rrow = CE(nc.sbuf_tensor(f"rrow_c_{b}", [1, 2 * TC], F32))
                trow = CE(nc.sbuf_tensor(f"trow_c_{b}", [1, 2 * TC], F32))
                ht_all = CE(nc.sbuf_tensor(f"ht_all_{b}", [128, KC * 2 * TC], BF16))
                lg_sb = CE(nc.sbuf_tensor(f"lg_sb_{b}", [8, TC], F32))
                lgtm = CE(nc.sbuf_tensor(f"lgtm_{b}", [128, NTT * 8], F32))
                tk = CE(nc.sbuf_tensor(f"tk_{b}", [128, NTT * 20], F32))
                gtm = CE(nc.sbuf_tensor(f"gtm_{b}", [128, NTT * 8], F32))
                gfm = CE(nc.sbuf_tensor(f"gfm_{b}", [8, TC], F32))
                gfm_bf = CE(nc.sbuf_tensor(f"gfm_bf_{b}", [8, TC], BF16))
                wm = [CE(nc.sbuf_tensor(f"wmc{i}_{b}", [128, EC * 128], BF16)) for i in range(2)]
                b1row = CE(nc.sbuf_tensor(f"b1row_{b}", [1, EDIM], BF16))
                ones_tc = CE(nc.sbuf_tensor(f"ones_tc_{b}", [1, TC], BF16))
                ghid = CE(nc.sbuf_tensor(f"ghid_{b}", [128, 2 * EC * TC], BF16))
                sqg = CE(nc.sbuf_tensor(f"sqg_{b}", [128, TC], F32))
                ug = CE(nc.sbuf_tensor(f"ug_{b}", [128, TC], F32))
                sg = CE(nc.sbuf_tensor(f"sg_{b}", [128, TC], F32))
                gbc = [CE(nc.sbuf_tensor(f"gbc{i}_{b}", [128, TC], F32)) for i in range(2)]
                moe_sb = CE(nc.sbuf_tensor(f"moe_sb_{b}", [128, 2 * KC * TC], F32))
                ft = [CE(nc.sbuf_tensor(f"ftc{i}_{b}", [128, TC], F32)) for i in range(2)]
                f_sb = CE(nc.sbuf_tensor(f"f_sb_{b}", [128, KC * TC], F32))
                ps_mm = [CE(nc.psum_tensor(f"ps_cm{i}_{b}", [128, TC], F32)) for i in range(2)]
                ps_st = CE(nc.psum_tensor(f"ps_cst_{b}", [128, 2 * TC], F32))
                ps_tr = [CE(nc.psum_tensor(f"ps_trc_{b}", [128, 128], F32))]
                ps_lg = CE(nc.psum_tensor(f"ps_lg_{b}", [8, TC], F32))
                ps_b2 = CE(nc.psum_tensor(f"ps_b2_{b}", [128, TC], F32))

                P.op("vector", lambda e: e.memset(ones_tc[:], 1.0), ve)
                for th in range(2):
                    tbs = [2 * th, 2 * th + 1]
                    for j, tb in enumerate(tbs):
                        t0 = tb * TC
                        emit_x_load(b, tb, x_fm, xtm, ps_tr)
                        P.wait("sync", ve)
                        for kc in range(KC):
                            dma(mix_fm[:, kc * TC:(kc + 1) * TC], s_mix[b, kc, :, t0:t0 + TC], lmix)
                        P.wait("vector", lmix)
                        P.wait("vector", stb)
                        for kc in range(KC):
                            P.op("vector", lambda e, kc=kc: e.tensor_add(
                                out=x_fm[:, kc * TC:(kc + 1) * TC],
                                in0=x_fm[:, kc * TC:(kc + 1) * TC],
                                in1=mix_fm[:, kc * TC:(kc + 1) * TC]), ve)
                        emit_rmsnorm(x_fm, n2w,
                                     lambda kc, j=j: ht_all[:, (kc * 2 + j) * TC:(kc * 2 + j + 1) * TC],
                                     sq_bf, ps_st, ps_mm[0], rrow)
                        P.wait("tensor", ve)
                        for kc in range(KC):
                            P.op("tensor", lambda e, kc=kc, j=j: e.matmul(
                                out=ps_lg[:, 0:TC], lhsT=rww_bf[:, kc * 8:(kc + 1) * 8],
                                rhs=ht_all[:, (kc * 2 + j) * TC:(kc * 2 + j + 1) * TC],
                                start=(kc == 0), stop=(kc == KC - 1)), pe)
                        P.wait("vector", pe)
                        P.op("vector", lambda e: e.tensor_copy(out=lg_sb[:], in_=ps_lg[0:8, 0:TC]), ve)
                        emit_logit_transpose(lg_sb, lgtm, ps_tr)
                        emit_topk_gates(lgtm, gtm, tk, se)
                        P.wait("vector", stc)
                        emit_rows_back(gtm, gfm, 8, ps_tr)
                        P.wait("sync", ve)
                        store(s_gates[b, tb], gfm[:], stc)
                    P.op("vector", lambda e: e.memset(moe_sb[:], 0.0), ve)
                    w1v = s_w1bf.rearrange("e (c p) m -> e p c m", p=128)
                    w2v = s_w2bf.rearrange("e (c p) m -> e p c m", p=128)
                    for e_i in range(NE):
                        P.wait("gpsimd", pe)
                        dma(b1row[:], w["b1"][e_i:e_i + 1, :], gd, q="gpsimd")
                        for hc in range(EC):
                            iw = hc % 2
                            P.wait("sync", pe)
                            wld = dma(wm[iw][:, 0:KC * 128].rearrange("p (c n) -> p c n", n=128),
                                      w1v[e_i][:, :, hc * 128:(hc + 1) * 128], lw)
                            for j, tb in enumerate(tbs):
                                pb = ps_mm[j]
                                P.wait("tensor", lw)
                                P.wait("tensor", gd)
                                P.wait("tensor", se)
                                P.wait("tensor", ve)
                                for kc in range(KC):
                                    P.op("tensor", lambda e, hc=hc, kc=kc, j=j, iw=iw, pb=pb: e.matmul(
                                        out=pb[:, 0:TC],
                                        lhsT=wm[iw][:, kc * 128:(kc + 1) * 128],
                                        rhs=ht_all[:, (kc * 2 + j) * TC:(kc * 2 + j + 1) * TC],
                                        start=(kc == 0), stop=False), pe)
                                P.op("tensor", lambda e, hc=hc, pb=pb: e.matmul(
                                    out=pb[:, 0:TC], lhsT=b1row[0:1, hc * 128:(hc + 1) * 128],
                                    rhs=ones_tc[0:1, 0:TC], start=False, stop=True), pe)
                                P.wait("scalar", pe)
                                P.wait("scalar", ve)
                                sqv = P.op("scalar", lambda e, pb=pb: e.activation(
                                    out=sqg[:], in_=pb[:, 0:TC], func=AF.Square, scale=SQRT_GK2), se)
                                P.wait("vector", se, sqv)
                                ugv = P.op("vector", lambda e, pb=pb: e.scalar_tensor_tensor(
                                    out=ug[:], in0=sqg[:], scalar=GK1, in1=pb[:, 0:TC],
                                    op0=ALU.add, op1=ALU.mult), ve)
                                P.wait("scalar", ve, ugv)
                                sgv = P.op("scalar", lambda e: e.activation(
                                    out=sg[:], in_=ug[:], func=AF.Sigmoid), se)
                                P.wait("vector", se, sgv)
                                P.op("vector", lambda e, j=j, hc=hc, pb=pb: e.tensor_mul(
                                    out=ghid[:, (j * EC + hc) * TC:(j * EC + hc + 1) * TC],
                                    in0=sg[:], in1=pb[:, 0:TC]), ve)
                        P.wait("sync", ve)
                        P.wait("sync", stc)
                        for j, tb in enumerate(tbs):
                            dma(gbc[j][:], s_gates[b, tb, e_i:e_i + 1, :].partition_broadcast(128), lbc)
                        for m in range(KC):
                            iw = m % 2
                            P.wait("sync", pe)
                            wld = dma(wm[iw][:, 0:EC * 128].rearrange("p (c n) -> p c n", n=128),
                                      w2v[e_i][:, :, m * 128:(m + 1) * 128], lw)
                            for j, tb in enumerate(tbs):
                                pb = ps_mm[j]
                                P.wait("tensor", lw)
                                P.wait("tensor", ve)
                                for hc in range(EC):
                                    P.op("tensor", lambda e, m=m, hc=hc, j=j, iw=iw, pb=pb: e.matmul(
                                        out=pb[:, 0:TC],
                                        lhsT=wm[iw][:, hc * 128:(hc + 1) * 128],
                                        rhs=ghid[:, (j * EC + hc) * TC:(j * EC + hc + 1) * TC],
                                        start=(hc == 0), stop=(hc == EC - 1)), pe)
                                P.wait("vector", pe)
                                P.wait("vector", lbc)
                                P.op("vector", lambda e, j=j, pb=pb: e.tensor_mul(
                                    out=ft[0][:], in0=pb[:, 0:TC], in1=gbc[j][:]), ve)
                                mv = moe_sb[:, (j * KC + m) * TC:(j * KC + m + 1) * TC]
                                P.op("vector", lambda e, mv=mv: e.tensor_add(
                                    out=mv, in0=mv, in1=ft[0][:]), ve)
                    for j, tb in enumerate(tbs):
                        t0 = tb * TC
                        emit_x_load(b, tb, x_fm, xtm, ps_tr)
                        P.wait("sync", ve)
                        for kc in range(KC):
                            dma(mix_fm[:, kc * TC:(kc + 1) * TC], s_mix[b, kc, :, t0:t0 + TC], lmix)
                        P.wait("sync", stc)
                        dma(gfm[:], s_gates[b, tb], lbc)
                        P.wait("vector", lbc)
                        P.op("vector", lambda e: e.tensor_copy(out=gfm_bf[:], in_=gfm[:]), ve)
                        P.wait("tensor", ve)
                        for m in range(KC):
                            P.op("tensor", lambda e, m=m: e.matmul(
                                out=ps_b2[:, 0:TC], lhsT=b2_bf[:, m * 128:(m + 1) * 128],
                                rhs=gfm_bf[:], start=True, stop=True), pe)
                            P.wait("vector", pe)
                            mv = moe_sb[:, (j * KC + m) * TC:(j * KC + m + 1) * TC]
                            P.op("vector", lambda e, mv=mv: e.tensor_add(
                                out=mv, in0=mv, in1=ps_b2[:, 0:TC]), ve)
                            P.wait("tensor", ve)
                            P.op("vector", lambda e, m=m: e.scalar_tensor_tensor(
                                out=x_fm[:, m * TC:(m + 1) * TC],
                                in0=x_fm[:, m * TC:(m + 1) * TC], scalar=2.0,
                                in1=mix_fm[:, m * TC:(m + 1) * TC],
                                op0=ALU.mult, op1=ALU.add), ve)
                            P.op("vector", lambda e, m=m, mv=mv: e.tensor_add(
                                out=x_fm[:, m * TC:(m + 1) * TC],
                                in0=x_fm[:, m * TC:(m + 1) * TC], in1=mv), ve)
                        emit_layernorm(x_fm, mix_fm, x2_bf, sq_bf, ps_st, ps_mm, rrow, trow)
                        P.op("vector", lambda e: e.tensor_copy(out=x2_bf[:], in_=mix_fm[:]), ve)
                        P.wait("tensor", ve)
                        for kc in range(KC):
                            P.op("tensor", lambda e, kc=kc: e.matmul(
                                out=ps_lg[:, 0:TC], lhsT=gww_bf[:, kc * 8:(kc + 1) * 8],
                                rhs=x2_bf[:, kc * TC:(kc + 1) * TC],
                                start=(kc == 0), stop=(kc == KC - 1)), pe)
                        P.wait("vector", pe)
                        P.op("vector", lambda e: e.tensor_copy(out=lg_sb[:], in_=ps_lg[0:8, 0:TC]), ve)
                        emit_logit_transpose(lg_sb, lgtm, ps_tr)
                        col = lambda tt, i: tk[:, tt * 20 + i: tt * 20 + i + 1]
                        lgm_ = lambda tt: tk[:, tt * 20 + 9: tt * 20 + 17]
                        lg8_ = lambda tt: lgtm[:, tt * 8:(tt + 1) * 8]
                        g8_ = lambda tt: gtm[:, tt * 8:(tt + 1) * 8]
                        for tt in range(NTT):
                            P.op("vector", lambda e, tt=tt: e.reduce_max(
                                out=col(tt, 0), in_=lg8_(tt), axis=AX), ve)
                        for tt in range(NTT):
                            P.op("vector", lambda e, tt=tt: e.tensor_scalar(
                                out=g8_(tt), in0=lg8_(tt), scalar1=col(tt, 0),
                                scalar2=None, op0=ALU.is_equal), ve)
                        for tt in range(NTT):
                            P.op("vector", lambda e, tt=tt: e.scalar_tensor_tensor(
                                out=lgm_(tt), in0=g8_(tt), scalar=-1e30, in1=lg8_(tt),
                                op0=ALU.mult, op1=ALU.add), ve)
                        for tt in range(NTT):
                            P.op("vector", lambda e, tt=tt: e.reduce_max(
                                out=col(tt, 1), in_=lgm_(tt), axis=AX), ve)
                        for tt in range(NTT):
                            P.op("vector", lambda e, tt=tt: e.tensor_sub(
                                out=col(tt, 2), in0=col(tt, 1), in1=col(tt, 0)), ve)
                        for tt in range(NTT):
                            P.op("vector", lambda e, tt=tt: e.tensor_copy(
                                out=col(tt, 7), in_=g8_(tt)[:, 0:1]), ve)
                        for tt in range(NTT):
                            P.op("vector", lambda e, tt=tt: e.tensor_scalar(
                                out=col(tt, 8), in0=lgm_(tt)[:, 1:2], scalar1=col(tt, 1),
                                scalar2=None, op0=ALU.is_equal), ve)
                        for tt in range(NTT):
                            nmv = P.op("vector", lambda e, tt=tt: e.tensor_scalar_mul(
                                out=col(tt, 3), in0=col(tt, 0), scalar1=-1.0), ve)
                        P.wait("scalar", ve, nmv)
                        for tt in range(NTT):
                            P.op("scalar", lambda e, tt=tt: e.activation(
                                out=lgm_(tt), in_=lg8_(tt), func=AF.Exp,
                                bias=col(tt, 3)), se)
                        for tt in range(NTT):
                            ex2 = P.op("scalar", lambda e, tt=tt: e.activation(
                                out=col(tt, 4), in_=col(tt, 2), func=AF.Exp), se)
                        P.wait("vector", se, ex2)
                        for tt in range(NTT):
                            zv = P.op("vector", lambda e, tt=tt: e.reduce_sum(
                                out=col(tt, 5), in_=lgm_(tt), axis=AX), ve)
                        P.wait("scalar", ve, zv)
                        for tt in range(NTT):
                            P.op("scalar", lambda e, tt=tt: e.activation(
                                out=col(tt, 5), in_=col(tt, 5), func=AF.Ln), se)
                        for tt in range(NTT):
                            rzv = P.op("scalar", lambda e, tt=tt: e.activation(
                                out=col(tt, 6), in_=col(tt, 5), func=AF.Exp,
                                scale=-1.0), se)
                        P.wait("vector", se, rzv)
                        for tt in range(NTT):
                            P.op("vector", lambda e, tt=tt: e.tensor_mul(
                                out=col(tt, 7), in0=col(tt, 7), in1=col(tt, 6)), ve)
                        for tt in range(NTT):
                            P.op("vector", lambda e, tt=tt: e.tensor_mul(
                                out=col(tt, 8), in0=col(tt, 8), in1=col(tt, 4)), ve)
                        for tt in range(NTT):
                            P.op("vector", lambda e, tt=tt: e.tensor_mul(
                                out=col(tt, 8), in0=col(tt, 8), in1=col(tt, 6)), ve)
                        for tt in range(NTT):
                            P.op("vector", lambda e, tt=tt: e.tensor_copy(
                                out=gtm[:, tt * 8 + 0: tt * 8 + 1], in_=col(tt, 7)), ve)
                        for tt in range(NTT):
                            P.op("vector", lambda e, tt=tt: e.tensor_copy(
                                out=gtm[:, tt * 8 + 1: tt * 8 + 2], in_=col(tt, 8)), ve)
                        P.wait("vector", stc)
                        emit_rows_back(gtm, gfm, 2, ps_tr)
                        P.wait("sync", ve)
                        store(s_coef[b, tb], gfm[0:2, :], stc)
                        P.wait("sync", stc)
                        P.wait("sync", ve)
                        dma(gbc[0][:], s_coef[b, tb, 0:1, :].partition_broadcast(128), lbc)
                        dma(gbc[1][:], s_coef[b, tb, 1:2, :].partition_broadcast(128), lbc)
                        for ei, (sw, bsb) in enumerate(((s_w0t, fb0), (s_w1t, fb1))):
                            for m in range(KC):
                                iw = m % 2
                                P.wait("sync", pe)
                                wld = dma(wm[iw][:, 0:KC * 128].rearrange("p (k n) -> p k n", k=KC),
                                          sw[:, :, m * 128:(m + 1) * 128].rearrange("k p n -> p k n"), lw)
                                pb = ps_mm[m % 2]
                                P.wait("tensor", lw)
                                P.wait("tensor", se)
                                P.wait("tensor", ve)
                                for kc in range(KC):
                                    P.op("tensor", lambda e, m=m, kc=kc, iw=iw, pb=pb: e.matmul(
                                        out=pb[:, 0:TC],
                                        lhsT=wm[iw][:, kc * 128:(kc + 1) * 128],
                                        rhs=x2_bf[:, kc * TC:(kc + 1) * TC],
                                        start=(kc == 0), stop=(kc == KC - 1)), pe)
                                P.wait("scalar", pe)
                                P.wait("scalar", ve)
                                et = ft[m % 2]
                                P.op("scalar", lambda e, m=m, pb=pb, et=et, bsb=bsb: e.activation(
                                    out=et[:], in_=pb[:, 0:TC],
                                    func=AF.Silu, bias=bsb[:, m:m + 1]), se)
                                P.wait("vector", se)
                                P.wait("vector", lbc)
                                if ei == 0:
                                    P.op("vector", lambda e, m=m, et=et: e.tensor_mul(
                                        out=f_sb[:, m * TC:(m + 1) * TC], in0=et[:],
                                        in1=gbc[0][:]), ve)
                                else:
                                    P.op("vector", lambda e, m=m, et=et: e.tensor_mul(
                                        out=et[:], in0=et[:], in1=gbc[1][:]), ve)
                                    P.op("vector", lambda e, m=m, et=et: e.tensor_add(
                                        out=f_sb[:, m * TC:(m + 1) * TC],
                                        in0=f_sb[:, m * TC:(m + 1) * TC], in1=et[:]), ve)
                        for m in range(KC):
                            P.op("vector", lambda e, m=m: e.tensor_add(
                                out=f_sb[:, m * TC:(m + 1) * TC],
                                in0=f_sb[:, m * TC:(m + 1) * TC],
                                in1=mix_fm[:, m * TC:(m + 1) * TC]), ve)
                        emit_layernorm(f_sb, x_fm, x2_bf, sq_bf, ps_st, ps_mm, rrow, trow)
                        for tt in range(NTT):
                            P.wait("vector", stc)
                            for kc in range(KC):
                                P.wait("tensor", ve)
                                P.op("tensor", lambda e, tt=tt, kc=kc: e.transpose(
                                    out=ps_tr[0][:],
                                    in_=x_fm[:, kc * TC + tt * 128: kc * TC + (tt + 1) * 128],
                                    identity=id_f[:]), pe)
                                P.wait("vector", pe)
                                P.op("vector", lambda e, kc=kc: e.tensor_copy(
                                    out=xtm[1][:, kc * 128:(kc + 1) * 128], in_=ps_tr[0][:]), ve)
                            P.wait("sync", ve)
                            store(out[b, t0 + tt * 128: t0 + (tt + 1) * 128, :], xtm[1][:], stc)
                P.barrier(ALLSEMS)

        for b in range(NB):
            if "A" in phases:
                phase_A(b)
            if "B" in phases:
                phase_B(b)
            if "C" in phases:
                phase_C(b)

        for sm in ALLSEMS:
            P.wait("sync", sm)

        with nc.Block() as block:
            @block.sync
            def _(eng):
                for fn in P.streams["sync"]:
                    fn(eng)

            @block.vector
            def _(eng):
                for fn in P.streams["vector"]:
                    fn(eng)

            @block.scalar
            def _(eng):
                for fn in P.streams["scalar"]:
                    fn(eng)

            @block.tensor
            def _(eng):
                for fn in P.streams["tensor"]:
                    fn(eng)

            @block.gpsimd
            def _(eng):
                for fn in P.streams["gpsimd"]:
                    fn(eng)

    return nc


def kernel(**inputs):
    ncores = 8
    import ml_dtypes
    x_full = np.ascontiguousarray(np.asarray(inputs["L0_attn_x"], dtype=np.float32))
    base = {
        "norm1_w": inputs["L0_attn_norm1_w"], "norm2_w": inputs["L0_attn_norm2_w"],
        "in_proj_w": inputs["L0_attn_in_proj_w"], "conv_w": inputs["L0_attn_conv_w"],
        "conv_b": inputs["L0_attn_conv_b"], "x_proj_w": inputs["L0_attn_x_proj_w"],
        "dt_proj_w": inputs["L0_attn_dt_proj_w"], "dt_proj_b": inputs["L0_attn_dt_proj_b"],
        "A_log": inputs["L0_attn_A_log"], "D_param": inputs["L0_attn_D_param"],
        "out_proj_w": inputs["L0_attn_out_proj_w"], "router_w": inputs["L0_attn_router_w"],
        "w1": inputs["L0_attn_w1"], "b1": inputs["L0_attn_b1"],
        "w2": inputs["L0_attn_w2"], "b2": inputs["L0_attn_b2"],
        "gate_w": inputs["L0_ffn_gate_w"], "ffn_w0": inputs["L0_ffn_w0"],
        "ffn_b0": inputs["L0_ffn_b0"], "ffn_w1": inputs["L0_ffn_w1"],
        "ffn_b1": inputs["L0_ffn_b1"],
        "ident_bf": np.eye(128, dtype=ml_dtypes.bfloat16),
        "ident_f32": np.eye(128, dtype=np.float32),
    }
    base = {k: np.ascontiguousarray(np.asarray(v, dtype=(ml_dtypes.bfloat16 if k == "ident_bf" else np.float32)))
            for k, v in base.items()}
    nc = build_program()
    in_maps = []
    for c in range(ncores):
        m = dict(base)
        m["x"] = np.ascontiguousarray(x_full[c * NB:(c + 1) * NB])
        in_maps.append(m)
    res = run_bass_kernel_spmd(nc, in_maps, core_ids=list(range(ncores)))
    return np.concatenate([np.asarray(r["out"], np.float32) for r in res.results], axis=0)


# version 02
# speedup vs baseline: 1.6840x; 1.6840x over previous
import math
import numpy as np
import concourse.bass as bass
import concourse.mybir as mybir
from concourse.bass_utils import run_bass_kernel_spmd
from contextlib import ExitStack

F32 = mybir.dt.float32
BF16 = mybir.dt.bfloat16
AF = mybir.ActivationFunctionType
ALU = mybir.AluOpType
AX = mybir.AxisListType.X

NB = 4
L = 2048
D = 1024
DI = 2048
DS = 16
NE = 8
EDIM = 2048
TC = 512
NTB = L // TC
KC = D // 128
NDC = DI // 128
EC = EDIM // 128
NTT = TC // 128

GK1 = 2.0 * 0.7978845608028654
GK2 = GK1 * 0.044715
SQRT_GK2 = math.sqrt(GK2)

ENGINES = ("sync", "vector", "scalar", "tensor", "gpsimd")


class Sem:
    def __init__(self, stack, nc, name):
        self.h = stack.enter_context(nc.semaphore(name))
        self.n = 0


class Prog:
    def __init__(self):
        self.streams = {e: [] for e in ENGINES}

    def op(self, eng, fn, sem, k=1):
        sem.n += k

        def wrapped(e, fn=fn, sem=sem, k=k):
            fn(e).then_inc(sem.h, k)
        self.streams[eng].append(wrapped)
        return sem.n

    def wait(self, eng, sem, val=None):
        v = sem.n if val is None else val
        if v <= 0:
            return
        self.streams[eng].append(lambda e, s=sem, v=v: e.wait_ge(s.h, v))

    def barrier(self, sems):
        for eng in ENGINES:
            for s in sems:
                self.wait(eng, s)


def build_program(dbg=False, phases="ABC"):
    nc = bass.Bass("TRN2", target_bir_lowering=False)

    x_in = nc.dram_tensor("x", [NB, L, D], F32, kind="ExternalInput").ap()
    out = nc.dram_tensor("out", [NB, L, D], F32, kind="ExternalOutput").ap()

    w = {}
    for name, shape in [
        ("norm1_w", [D]), ("norm2_w", [D]), ("in_proj_w", [D, 2 * DI]),
        ("conv_w", [DI, 4]), ("conv_b", [DI]), ("x_proj_w", [DI, 96]),
        ("dt_proj_w", [64, DI]), ("dt_proj_b", [DI]), ("A_log", [DI, DS]),
        ("D_param", [DI]), ("out_proj_w", [DI, D]), ("router_w", [D, NE]),
        ("w1", [NE, D, EDIM]), ("b1", [NE, EDIM]), ("w2", [NE, EDIM, D]),
        ("b2", [NE, D]), ("gate_w", [D, NE]), ("ffn_w0", [D, D]),
        ("ffn_b0", [D]), ("ffn_w1", [D, D]), ("ffn_b1", [D]),
        ("ident_bf", [128, 128]), ("ident_f32", [128, 128]),
    ]:
        w[name] = nc.dram_tensor(name, shape, BF16 if name == "ident_bf" else F32,
                                 kind="ExternalInput").ap()

    skind = "ExternalOutput" if dbg else "Internal"
    s_xc = nc.dram_tensor("s_xc", [NB, NDC, 128, L], F32, kind=skind).ap()
    s_z = nc.dram_tensor("s_z", [NB, NDC, 128, L], F32, kind=skind).ap()
    s_dt = nc.dram_tensor("s_dt", [NB, NDC, 128, L], F32, kind=skind).ap()
    s_bc = nc.dram_tensor("s_bc", [NB, 2 * DS, L], F32, kind=skind).ap()
    s_mix = nc.dram_tensor("s_mix", [NB, KC, 128, L], F32, kind=skind).ap()
    s_gates = nc.dram_tensor("s_gates", [NB, NTB, NE, TC], F32, kind=skind).ap()
    s_coef = nc.dram_tensor("s_coef", [NB, NTB, 2, TC], F32, kind=skind).ap()
    s_ipbf = nc.dram_tensor("s_ipbf", [D, 2 * DI], BF16, kind="Internal").ap()
    s_opbf = nc.dram_tensor("s_opbf", [DI, D], BF16, kind="Internal").ap()
    s_w1bf = nc.dram_tensor("s_w1bf", [NE, D, EDIM], BF16, kind="Internal").ap()
    s_w2bf = nc.dram_tensor("s_w2bf", [NE, EDIM, D], BF16, kind="Internal").ap()
    s_w0t = nc.dram_tensor("s_w0t", [KC, 128, D], BF16, kind="Internal").ap()
    s_w1t = nc.dram_tensor("s_w1t", [KC, 128, D], BF16, kind="Internal").ap()

    P = Prog()

    with ExitStack() as ctx:
        E = ctx.enter_context
        lset = Sem(ctx, nc, "lset")
        gd = Sem(ctx, nc, "gd")
        lx = Sem(ctx, nc, "lx")
        lw = Sem(ctx, nc, "lw")
        lact = Sem(ctx, nc, "lact")
        lbc = Sem(ctx, nc, "lbc")
        lmix = Sem(ctx, nc, "lmix")
        lwt = Sem(ctx, nc, "lwt")
        sta = Sem(ctx, nc, "sta")
        lw2 = [Sem(ctx, nc, f"lw2_{i}") for i in range(2)]
        lact2 = [Sem(ctx, nc, f"lact2_{i}") for i in range(2)]
        stz2 = [Sem(ctx, nc, f"stz2_{i}") for i in range(2)]
        stx2 = [Sem(ctx, nc, f"stx2_{i}") for i in range(2)]
        std2 = [Sem(ctx, nc, f"std2_{i}") for i in range(2)]
        stb2 = [Sem(ctx, nc, f"stb2_{i}") for i in range(2)]
        stb = Sem(ctx, nc, "stb")
        stc = Sem(ctx, nc, "stc")
        pe = Sem(ctx, nc, "pe")
        ve = Sem(ctx, nc, "ve")
        se = Sem(ctx, nc, "se")
        ALLSEMS = ([lset, gd, lx, lw, lact, lbc, lmix, lwt, sta, stb, stc, pe, ve, se]
                   + lw2 + lact2 + stz2 + stx2 + std2 + stb2)

        def dma(dst, src, sem, q="sync", slow=False):
            return P.op(q, lambda e, d=dst, s=src, sl=slow: e.dma_start(
                out=d, in_=s, allow_slow_non_contiguous=sl), sem, 16)

        def store(dst, src, sem):
            return P.op("sync", lambda e, d=dst, s=src: e.dma_start(out=d, in_=s), sem, 16)

        ones_col_bf = E(nc.sbuf_tensor("ones_col_bf", [128, 1], BF16))
        ones_row_f = E(nc.sbuf_tensor("ones_row_f", [1, 128], F32))
        id_bf = E(nc.sbuf_tensor("id_bf", [128, 128], BF16))
        id_f = E(nc.sbuf_tensor("id_f", [128, 128], F32))
        n1w = E(nc.sbuf_tensor("n1w", [128, KC], F32))
        n2w = E(nc.sbuf_tensor("n2w", [128, KC], F32))
        convw = E(nc.sbuf_tensor("convw", [128, NDC * 4], F32))
        convb = E(nc.sbuf_tensor("convb", [128, NDC], F32))
        dtb = E(nc.sbuf_tensor("dtb", [128, NDC], F32))
        A_sb = E(nc.sbuf_tensor("A_sb", [128, NDC * DS], F32))
        D_sb = E(nc.sbuf_tensor("D_sb", [128, NDC], F32))
        b1_sb = E(nc.sbuf_tensor("b1_sb", [128, NE * EC], F32))
        b2_bf = E(nc.sbuf_tensor("b2_bf", [NE, D], BF16))
        fb0 = E(nc.sbuf_tensor("fb0", [128, KC], F32))
        fb1 = E(nc.sbuf_tensor("fb1", [128, KC], F32))
        xpw_bf = E(nc.sbuf_tensor("xpw_bf", [128, NDC * 96], BF16))
        dtw_bf = E(nc.sbuf_tensor("dtw_bf", [64, DI], BF16))
        rww_bf = E(nc.sbuf_tensor("rww_bf", [128, KC * NE], BF16))
        gww_bf = E(nc.sbuf_tensor("gww_bf", [128, KC * NE], BF16))
        state_sb = E(nc.sbuf_tensor("state_sb", [128, NDC * DS], F32))
        hist_bf = E(nc.sbuf_tensor("hist_bf", [128, NDC * 3], BF16))

        const_sb = E(nc.sbuf_tensor("const_sb", [128, 4], F32))
        for col, val in ((0, 1e-6), (1, 1e-5), (2, 1.0)):
            P.op("vector", lambda e, col=col, val=val: e.memset(const_sb[:, col:col + 1], val), ve)
            nc.const_aps.register(F32, val, const_sb[:, col:col + 1])

        dma(n1w[:], w["norm1_w"].rearrange("(c p) -> p c", p=128), lset, slow=True)
        dma(n2w[:], w["norm2_w"].rearrange("(c p) -> p c", p=128), lset, slow=True)
        dma(convw[:].rearrange("p (c k) -> p c k", k=4), w["conv_w"].rearrange("(c p) k -> p c k", p=128), lset)
        dma(convb[:], w["conv_b"].rearrange("(c p) -> p c", p=128), lset, slow=True)
        dma(dtb[:], w["dt_proj_b"].rearrange("(c p) -> p c", p=128), lset, slow=True)
        dma(D_sb[:], w["D_param"].rearrange("(c p) -> p c", p=128), lset, slow=True)
        dma(b1_sb[:].rearrange("p (e c) -> p e c", e=NE), w["b1"].rearrange("e (c p) -> p e c", p=128), lset, slow=True)
        dma(fb0[:], w["ffn_b0"].rearrange("(c p) -> p c", p=128), lset, slow=True)
        dma(fb1[:], w["ffn_b1"].rearrange("(c p) -> p c", p=128), lset, slow=True)
        dma(id_bf[:], w["ident_bf"], lset)
        dma(id_f[:], w["ident_f32"], lset)
        dma(A_sb[:].rearrange("p (c s) -> p c s", s=DS), w["A_log"].rearrange("(c p) s -> p c s", p=128), lset)
        dma(b2_bf[:], w["b2"], gd, q="gpsimd")
        dma(xpw_bf[:].rearrange("p (c m) -> p c m", m=96), w["x_proj_w"].rearrange("(c p) m -> p c m", p=128), gd, q="gpsimd")
        dma(dtw_bf[:], w["dt_proj_w"], gd, q="gpsimd")
        dma(rww_bf[:].rearrange("p (c m) -> p c m", m=NE), w["router_w"].rearrange("(c p) m -> p c m", p=128), gd, q="gpsimd")
        dma(gww_bf[:].rearrange("p (c m) -> p c m", m=NE), w["gate_w"].rearrange("(c p) m -> p c m", p=128), gd, q="gpsimd")
        dma(s_ipbf, w["in_proj_w"], gd, q="gpsimd")
        dma(s_opbf, w["out_proj_w"], gd, q="gpsimd")
        for e_i in range(NE):
            dma(s_w1bf[e_i], w["w1"][e_i], gd, q="gpsimd")
            dma(s_w2bf[e_i], w["w2"][e_i], gd, q="gpsimd")
        P.wait("scalar", lset)
        P.op("scalar", lambda e: e.activation(out=A_sb[:], in_=A_sb[:], func=AF.Exp), se)
        P.wait("vector", se)
        P.op("vector", lambda e: e.tensor_scalar_mul(out=A_sb[:], in0=A_sb[:], scalar1=-1.0), ve)
        P.wait("vector", lset)
        P.op("vector", lambda e: e.memset(ones_col_bf[:], 1.0), ve)
        P.op("vector", lambda e: e.memset(ones_row_f[:], 1.0), ve)

        with nc.psum_tensor("ps_t0", [128, 128], BF16) as ps_t0, \
                nc.sbuf_tensor("wtmp", [128, D], BF16) as wtmp, \
                nc.sbuf_tensor("wttmp", [128, D], BF16) as wttmp:
            for wsrc, sdst in ((w["ffn_w0"], s_w0t), (w["ffn_w1"], s_w1t)):
                for mc in range(KC):
                    P.wait("gpsimd", sta)
                    dma(wtmp[:], wsrc[mc * 128:(mc + 1) * 128, :], lwt, q="gpsimd")
                    for kc2 in range(KC):
                        P.wait("tensor", lwt)
                        P.wait("tensor", ve)
                        P.op("tensor", lambda e, kc2=kc2: e.transpose(
                            out=ps_t0[:], in_=wtmp[:, kc2 * 128:(kc2 + 1) * 128],
                            identity=id_bf[:]), pe)
                        P.wait("vector", pe)
                        P.op("vector", lambda e, kc2=kc2: e.tensor_copy(
                            out=wttmp[:, kc2 * 128:(kc2 + 1) * 128], in_=ps_t0[:]), ve)
                    P.wait("sync", ve)
                    for kc2 in range(KC):
                        store(sdst[kc2, :, mc * 128:(mc + 1) * 128],
                              wttmp[:, kc2 * 128:(kc2 + 1) * 128], sta)
        P.barrier(ALLSEMS)

        def emit_x_load(b, tb, x_fm, xtm, ps_tr):
            t0 = tb * TC
            P.wait("sync", stc)
            cp_ev = {}
            for tt in range(NTT):
                P.wait("sync", pe)
                lv = dma(xtm[tt % 2][:], x_in[b, t0 + tt * 128: t0 + (tt + 1) * 128, :], lx)
                for kc in range(KC):
                    i = tt * KC + kc
                    P.wait("tensor", lx, lv)
                    if i >= 2:
                        P.wait("tensor", ve, cp_ev[i - 2])
                    tr = P.op("tensor", lambda e, tt=tt, kc=kc, i=i: e.transpose(
                        out=ps_tr[i % 2][:], in_=xtm[tt % 2][:, kc * 128:(kc + 1) * 128],
                        identity=id_f[:]), pe)
                    P.wait("vector", pe, tr)
                    cp_ev[i] = P.op("vector", lambda e, tt=tt, kc=kc, i=i: e.tensor_copy(
                        out=x_fm[:, kc * TC + tt * 128: kc * TC + (tt + 1) * 128],
                        in_=ps_tr[i % 2][:]), ve)

        def emit_rmsnorm(x_fm, wsb, out_fn, sq_bf, ps_stat, ps_bcst, rrow, eps=1e-6):
            P.wait("scalar", ve)
            P.op("scalar", lambda e: e.activation(out=sq_bf[:], in_=x_fm[:], func=AF.Square), se)
            P.wait("tensor", se)
            P.wait("tensor", ve)
            for kc in range(KC):
                P.op("tensor", lambda e, kc=kc: e.matmul(
                    out=ps_stat[0:1, 0:TC], lhsT=ones_col_bf[:],
                    rhs=sq_bf[:, kc * TC:(kc + 1) * TC],
                    start=(kc == 0), stop=(kc == KC - 1)), pe)
            P.wait("scalar", pe)
            P.op("scalar", lambda e: e.activation(out=rrow[0:1, 0:TC], in_=ps_stat[0:1, 0:TC],
                                                  func=AF.Ln, scale=1.0 / D, bias=eps), se)
            P.op("scalar", lambda e: e.activation(out=rrow[0:1, 0:TC], in_=rrow[0:1, 0:TC],
                                                  func=AF.Exp, scale=-0.5), se)
            P.wait("tensor", se)
            P.op("tensor", lambda e: e.matmul(out=ps_bcst[:, 0:TC], lhsT=ones_row_f[:],
                                              rhs=rrow[0:1, 0:TC], start=True, stop=True), pe)
            P.wait("vector", pe)
            for kc in range(KC):
                P.op("vector", lambda e, kc=kc: e.scalar_tensor_tensor(
                    out=out_fn(kc), in0=x_fm[:, kc * TC:(kc + 1) * TC],
                    scalar=wsb[:, kc:kc + 1], in1=ps_bcst[:, 0:TC],
                    op0=ALU.mult, op1=ALU.mult), ve)

        def emit_layernorm(xin, out_f, cast_bf, sq_bf, ps_stat, ps_bc, rows, trow, eps=1e-5):
            P.wait("vector", se)
            P.op("vector", lambda e: e.tensor_copy(out=cast_bf[:], in_=xin[:]), ve)
            P.wait("scalar", ve)
            P.op("scalar", lambda e: e.activation(out=sq_bf[:], in_=xin[:], func=AF.Square), se)
            P.wait("tensor", se)
            P.wait("tensor", ve)
            for kc in range(KC):
                P.op("tensor", lambda e, kc=kc: e.matmul(
                    out=ps_stat[0:1, 0:TC], lhsT=ones_col_bf[:],
                    rhs=cast_bf[:, kc * TC:(kc + 1) * TC],
                    start=(kc == 0), stop=(kc == KC - 1)), pe)
            for kc in range(KC):
                P.op("tensor", lambda e, kc=kc: e.matmul(
                    out=ps_stat[0:1, TC:2 * TC], lhsT=ones_col_bf[:],
                    rhs=sq_bf[:, kc * TC:(kc + 1) * TC],
                    start=(kc == 0), stop=(kc == KC - 1)), pe)
            P.wait("scalar", pe)
            P.op("scalar", lambda e: e.activation(out=rows[0:1, 0:TC], in_=ps_stat[0:1, 0:TC],
                                                  func=AF.Copy, scale=1.0 / D), se)
            P.op("scalar", lambda e: e.activation(out=rows[0:1, TC:2 * TC],
                                                  in_=ps_stat[0:1, TC:2 * TC],
                                                  func=AF.Copy, scale=1.0 / D), se)
            P.wait("vector", se)
            P.op("vector", lambda e: e.tensor_mul(out=trow[0:1, 0:TC], in0=rows[0:1, 0:TC],
                                                  in1=rows[0:1, 0:TC]), ve)
            P.op("vector", lambda e: e.tensor_sub(out=trow[0:1, 0:TC],
                                                  in0=rows[0:1, TC:2 * TC],
                                                  in1=trow[0:1, 0:TC]), ve)
            P.wait("scalar", ve)
            P.op("scalar", lambda e: e.activation(out=trow[0:1, 0:TC],
                                                  in_=trow[0:1, 0:TC],
                                                  func=AF.Ln, bias=eps), se)
            P.op("scalar", lambda e: e.activation(out=trow[0:1, 0:TC],
                                                  in_=trow[0:1, 0:TC],
                                                  func=AF.Exp, scale=-0.5), se)
            P.wait("vector", se)
            P.op("vector", lambda e: e.tensor_mul(out=rows[0:1, TC:2 * TC],
                                                  in0=rows[0:1, 0:TC],
                                                  in1=trow[0:1, 0:TC]), ve)
            P.op("vector", lambda e: e.tensor_copy(out=rows[0:1, 0:TC],
                                                   in_=trow[0:1, 0:TC]), ve)
            P.wait("tensor", ve)
            P.op("tensor", lambda e: e.matmul(out=ps_bc[0][:, 0:TC], lhsT=ones_row_f[:],
                                              rhs=rows[0:1, 0:TC], start=True, stop=True), pe)
            P.op("tensor", lambda e: e.matmul(out=ps_bc[1][:, 0:TC], lhsT=ones_row_f[:],
                                              rhs=rows[0:1, TC:2 * TC], start=True, stop=True), pe)
            P.wait("vector", pe)
            for kc in range(KC):
                P.op("vector", lambda e, kc=kc: e.tensor_mul(
                    out=out_f[:, kc * TC:(kc + 1) * TC], in0=xin[:, kc * TC:(kc + 1) * TC],
                    in1=ps_bc[0][:, 0:TC]), ve)
            for kc in range(KC):
                P.op("vector", lambda e, kc=kc: e.tensor_sub(
                    out=out_f[:, kc * TC:(kc + 1) * TC], in0=out_f[:, kc * TC:(kc + 1) * TC],
                    in1=ps_bc[1][:, 0:TC]), ve)

        def emit_topk_gates(lgtm, gtm, tk, se_ref):
            col = lambda tt, i: tk[:, tt * 20 + i: tt * 20 + i + 1]
            lgm_ = lambda tt: tk[:, tt * 20 + 9: tt * 20 + 17]
            lg8_ = lambda tt: lgtm[:, tt * 8:(tt + 1) * 8]
            g8_ = lambda tt: gtm[:, tt * 8:(tt + 1) * 8]
            for tt in range(NTT):
                P.op("vector", lambda e, tt=tt: e.reduce_max(
                    out=col(tt, 0), in_=lg8_(tt), axis=AX), ve)
            for tt in range(NTT):
                P.op("vector", lambda e, tt=tt: e.tensor_scalar(
                    out=g8_(tt), in0=lg8_(tt), scalar1=col(tt, 0), scalar2=None,
                    op0=ALU.is_equal), ve)
            for tt in range(NTT):
                P.op("vector", lambda e, tt=tt: e.scalar_tensor_tensor(
                    out=lgm_(tt), in0=g8_(tt), scalar=-1e30, in1=lg8_(tt),
                    op0=ALU.mult, op1=ALU.add), ve)
            for tt in range(NTT):
                P.op("vector", lambda e, tt=tt: e.reduce_max(
                    out=col(tt, 1), in_=lgm_(tt), axis=AX), ve)
            for tt in range(NTT):
                dv = P.op("vector", lambda e, tt=tt: e.tensor_sub(
                    out=col(tt, 2), in0=col(tt, 1), in1=col(tt, 0)), ve)
            P.wait("scalar", ve, dv)
            for tt in range(NTT):
                sv = P.op("scalar", lambda e, tt=tt: e.activation(
                    out=col(tt, 3), in_=col(tt, 2), func=AF.Sigmoid, scale=-1.0), se)
            P.wait("vector", se, sv)
            for tt in range(NTT):
                P.op("vector", lambda e, tt=tt: e.tensor_scalar(
                    out=col(tt, 4), in0=col(tt, 3), scalar1=-1.0, scalar2=1.0,
                    op0=ALU.mult, op1=ALU.add), ve)
            for tt in range(NTT):
                P.op("vector", lambda e, tt=tt: e.tensor_scalar_mul(
                    out=g8_(tt), in0=g8_(tt), scalar1=col(tt, 3)), ve)
            for tt in range(NTT):
                P.op("vector", lambda e, tt=tt: e.tensor_scalar(
                    out=lgm_(tt), in0=lgm_(tt), scalar1=col(tt, 1), scalar2=None,
                    op0=ALU.is_equal), ve)
            for tt in range(NTT):
                P.op("vector", lambda e, tt=tt: e.scalar_tensor_tensor(
                    out=g8_(tt), in0=lgm_(tt), scalar=col(tt, 4), in1=g8_(tt),
                    op0=ALU.mult, op1=ALU.add), ve)

        def emit_logit_transpose(lg_sb, lgtm, ps_tr):
            P.wait("tensor", ve)
            for tt in range(NTT):
                P.op("tensor", lambda e, tt=tt: e.transpose(
                    out=ps_tr[0][0:128, 0:8], in_=lg_sb[:, tt * 128:(tt + 1) * 128],
                    identity=id_f[0:8, 0:8]), pe)
                P.wait("vector", pe)
                P.op("vector", lambda e, tt=tt: e.tensor_copy(
                    out=lgtm[:, tt * 8:(tt + 1) * 8], in_=ps_tr[0][0:128, 0:8]), ve)
                P.wait("tensor", ve)

        def emit_rows_back(gtm, gfm, nrows, ps_tr):
            P.wait("tensor", ve)
            for tt in range(NTT):
                P.op("tensor", lambda e, tt=tt: e.transpose(
                    out=ps_tr[0][0:8, 0:128], in_=gtm[:, tt * 8:(tt + 1) * 8],
                    identity=id_f[:]), pe)
                P.wait("vector", pe)
                P.op("vector", lambda e, tt=tt, nrows=nrows: e.tensor_copy(
                    out=gfm[0:nrows, tt * 128:(tt + 1) * 128],
                    in_=ps_tr[0][0:nrows, 0:128]), ve)
                P.wait("tensor", ve)

        def phase_A(b):
            with ExitStack() as actx:
                AE = actx.enter_context
                inproj_sb = AE(nc.sbuf_tensor(f"inproj_sb_{b}", [128, KC * 4096], BF16))
                diag_all = AE(nc.sbuf_tensor(f"diag_all_{b}", [128, NDC * 4 * 128], BF16))
                x_fm = AE(nc.sbuf_tensor(f"xa_fm_{b}", [128, KC * TC], F32))
                xtm = [AE(nc.sbuf_tensor(f"xtm{i}_{b}", [128, D], F32)) for i in range(2)]
                sq_bf = AE(nc.sbuf_tensor(f"sqa_bf_{b}", [128, KC * TC], BF16))
                h_bf = AE(nc.sbuf_tensor(f"h_bf_{b}", [128, KC * TC], BF16))
                rrow = AE(nc.sbuf_tensor(f"rrow_a_{b}", [1, 2 * TC], F32))
                xcraw = AE(nc.sbuf_tensor(f"xcraw_{b}", [128, NDC * TC], BF16))
                xcs_bf = AE(nc.sbuf_tensor(f"xcs_bf_{b}", [128, NDC * TC], BF16))
                ztile = [AE(nc.sbuf_tensor(f"ztile{i}_{b}", [128, TC], F32)) for i in range(2)]
                xtile = [AE(nc.sbuf_tensor(f"xtile{i}_{b}", [128, TC], F32)) for i in range(2)]
                dtile = [AE(nc.sbuf_tensor(f"dtile{i}_{b}", [128, TC], F32)) for i in range(2)]
                proj_sb = AE(nc.sbuf_tensor(f"proj_sb_{b}", [96, TC], F32))
                dtr_bf = AE(nc.sbuf_tensor(f"dtr_bf_{b}", [64, TC], BF16))
                sp1 = [AE(nc.sbuf_tensor(f"sp1{i}_{b}", [128, TC], F32)) for i in range(2)]
                sp2 = [AE(nc.sbuf_tensor(f"sp2{i}_{b}", [128, TC], F32)) for i in range(2)]
                spr = [AE(nc.sbuf_tensor(f"spr{i}_{b}", [128, TC], F32)) for i in range(2)]
                ps_mm = [AE(nc.psum_tensor(f"ps_mma{i}_{b}", [128, TC], F32)) for i in range(2)]
                ps_cv = [AE(nc.psum_tensor(f"ps_cv{i}_{b}", [128, TC], F32)) for i in range(2)]
                ps_xp = AE(nc.psum_tensor(f"ps_xp_{b}", [96, TC], F32))
                ps_st = AE(nc.psum_tensor(f"ps_sta_{b}", [128, TC], F32))
                ps_tr = [AE(nc.psum_tensor(f"ps_tra{i}_{b}", [128, 128], F32)) for i in range(2)]

                for kc in range(KC):
                    dma(inproj_sb[:, kc * 4096:(kc + 1) * 4096],
                        s_ipbf[kc * 128:(kc + 1) * 128, :], lw)
                P.wait("vector", lset)
                for dc in range(NDC):
                    for k in range(4):
                        P.op("vector", lambda e, dc=dc, k=k: e.tensor_scalar_mul(
                            out=diag_all[:, (dc * 4 + k) * 128:(dc * 4 + k + 1) * 128],
                            in0=id_bf[:], scalar1=convw[:, dc * 4 + k: dc * 4 + k + 1]), ve)
                hist0_ev = P.op("vector", lambda e: e.memset(hist_bf[:], 0.0), ve)
                hist_ev = {dc: hist0_ev for dc in range(NDC)}

                cv_evac_ev = {}
                for tb in range(NTB):
                    t0 = tb * TC
                    emit_x_load(b, tb, x_fm, xtm, ps_tr)
                    emit_rmsnorm(x_fm, n1w,
                                 lambda kc: h_bf[:, kc * TC:(kc + 1) * TC],
                                 sq_bf, ps_st, ps_mm[0], rrow)
                    norm_ev = ve.n
                    if tb == 0:
                        P.wait("tensor", lw)
                    ip_ev = {}
                    for m in range(32):
                        pb = ps_mm[m % 2]
                        if m < 2:
                            P.wait("tensor", ve, norm_ev)
                        else:
                            P.wait("tensor", se, ip_ev[m - 2])
                        mm_ev = None
                        for kc in range(KC):
                            mm_ev = P.op("tensor", lambda e, m=m, kc=kc, pb=pb: e.matmul(
                                out=pb[:, 0:TC],
                                lhsT=inproj_sb[:, kc * 4096 + m * 128: kc * 4096 + (m + 1) * 128],
                                rhs=h_bf[:, kc * TC:(kc + 1) * TC],
                                start=(kc == 0), stop=(kc == KC - 1)), pe)
                        P.wait("scalar", pe, mm_ev)
                        if m < 16:
                            ip_ev[m] = P.op("scalar", lambda e, m=m, pb=pb: e.activation(
                                out=xcraw[:, m * TC:(m + 1) * TC], in_=pb[:, 0:TC],
                                func=AF.Copy), se)
                        else:
                            zs = m % 2
                            zt = ztile[zs]
                            P.wait("scalar", stz2[zs])
                            ip_ev[m] = P.op("scalar", lambda e, pb=pb, zt=zt: e.activation(
                                out=zt[:], in_=pb[:, 0:TC], func=AF.Silu), se)
                            P.wait("sync", se, ip_ev[m])
                            store(s_z[b, m - 16, :, t0:t0 + TC], zt[:], stz2[zs])
                    xp_ev = None
                    for dc in range(NDC):
                        pcv = ps_cv[dc % 2]
                        cur = xcraw[:, dc * TC:(dc + 1) * TC]
                        hist = hist_bf[:, dc * 3:(dc + 1) * 3]
                        dgs = diag_all[:, dc * 4 * 128:(dc + 1) * 4 * 128]
                        P.wait("tensor", se, ip_ev[dc])
                        if dc >= 2:
                            P.wait("tensor", se, cv_evac_ev[dc - 2])
                        P.wait("tensor", ve, hist_ev[dc])
                        P.op("tensor", lambda e, cur=cur, dgs=dgs, pcv=pcv: e.matmul(
                            out=pcv[:, 0:TC], lhsT=dgs[:, 3 * 128:4 * 128], rhs=cur,
                            start=True, stop=False), pe)
                        cv_mm = None
                        for k in range(3):
                            nh = 3 - k
                            P.op("tensor", lambda e, k=k, nh=nh, hist=hist, dgs=dgs, pcv=pcv: e.matmul(
                                out=pcv[:, 0:nh], lhsT=dgs[:, k * 128:(k + 1) * 128],
                                rhs=hist[:, k:3], start=False, stop=False), pe)
                            cv_mm = P.op("tensor", lambda e, k=k, nh=nh, cur=cur, dgs=dgs, pcv=pcv: e.matmul(
                                out=pcv[:, nh:TC], lhsT=dgs[:, k * 128:(k + 1) * 128],
                                rhs=cur[:, 0:TC - nh], start=False, stop=(k == 2)), pe)
                        P.wait("vector", pe, cv_mm)
                        hist_ev[dc] = P.op("vector", lambda e, cur=cur, hist=hist: e.tensor_copy(
                            out=hist, in_=cur[:, TC - 3:TC]), ve)
                        xs = dc % 2
                        xt = xtile[xs]
                        P.wait("scalar", pe, cv_mm)
                        P.wait("scalar", stx2[xs])
                        cv_evac_ev[dc] = P.op("scalar", lambda e, dc=dc, xt=xt, pcv=pcv: e.activation(
                            out=xt[:], in_=pcv[:, 0:TC], func=AF.Silu,
                            bias=convb[:, dc:dc + 1]), se)
                        P.wait("sync", se, cv_evac_ev[dc])
                        store(s_xc[b, dc, :, t0:t0 + TC], xt[:], stx2[xs])
                        P.wait("vector", se, cv_evac_ev[dc])
                        xcs_ev = P.op("vector", lambda e, dc=dc, xt=xt: e.tensor_copy(
                            out=xcs_bf[:, dc * TC:(dc + 1) * TC], in_=xt[:]), ve)
                        P.wait("tensor", ve, xcs_ev)
                        xp_ev = P.op("tensor", lambda e, dc=dc: e.matmul(
                            out=ps_xp[:, 0:TC], lhsT=xpw_bf[:, dc * 96:(dc + 1) * 96],
                            rhs=xcs_bf[:, dc * TC:(dc + 1) * TC],
                            start=(dc == 0), stop=(dc == NDC - 1)), pe)
                    P.wait("vector", pe, xp_ev)
                    P.op("vector", lambda e: e.tensor_copy(out=proj_sb[:], in_=ps_xp[0:96, 0:TC]), ve)
                    dtr_ev = P.op("vector", lambda e: e.tensor_copy(out=dtr_bf[:], in_=proj_sb[0:64, :]), ve)
                    P.wait("sync", ve, dtr_ev)
                    P.wait("sync", sta)
                    store(s_bc[b, :, t0:t0 + TC], proj_sb[64:96, :], sta)
                    abs_ev, ln_ev, relu_ev, add_ev = {}, {}, {}, {}
                    for m in range(NDC):
                        ms = m % 2
                        pb = ps_mm[ms]
                        if m < 2:
                            P.wait("tensor", ve, dtr_ev)
                            P.wait("tensor", se, ip_ev[30 + m])
                        else:
                            P.wait("tensor", se, abs_ev[m - 2])
                            P.wait("tensor", ve, relu_ev[m - 2])
                        dmm = P.op("tensor", lambda e, m=m, pb=pb: e.matmul(
                            out=pb[:, 0:TC], lhsT=dtw_bf[:, m * 128:(m + 1) * 128],
                            rhs=dtr_bf[:], start=True, stop=True), pe)
                        P.wait("scalar", pe, dmm)
                        if m >= 2:
                            P.wait("scalar", ve, add_ev[m - 2])
                        abs_ev[m] = P.op("scalar", lambda e, m=m, pb=pb, ms=ms: e.activation(
                            out=sp1[ms][:], in_=pb[:, 0:TC], func=AF.Abs,
                            bias=dtb[:, m:m + 1]), se)
                        P.op("scalar", lambda e, ms=ms: e.activation(
                            out=sp2[ms][:], in_=sp1[ms][:], func=AF.Exp, scale=-1.0), se)
                        ln_ev[m] = P.op("scalar", lambda e, ms=ms: e.activation(
                            out=sp1[ms][:], in_=sp2[ms][:], func=AF.Ln, bias=1.0), se)
                        P.wait("vector", se, abs_ev[m])
                        relu_ev[m] = P.op("vector", lambda e, m=m, pb=pb, ms=ms: e.tensor_scalar(
                            out=spr[ms][:], in0=pb[:, 0:TC], scalar1=dtb[:, m:m + 1],
                            scalar2=0.0, op0=ALU.add, op1=ALU.max), ve)
                        P.wait("vector", se, ln_ev[m])
                        P.wait("vector", std2[ms])
                        dt_t = dtile[ms]
                        add_ev[m] = P.op("vector", lambda e, dt_t=dt_t, ms=ms: e.tensor_add(
                            out=dt_t[:], in0=spr[ms][:], in1=sp1[ms][:]), ve)
                        P.wait("sync", ve, add_ev[m])
                        store(s_dt[b, m, :, t0:t0 + TC], dt_t[:], std2[ms])
                P.barrier(ALLSEMS)

        def phase_B(b):
            with ExitStack() as bctx:
                BE = bctx.enter_context
                cache = BE(nc.sbuf_tensor(f"bc_cache_{b}", [128, 2 * DS * TC], F32))
                dtt = [BE(nc.sbuf_tensor(f"dtt{i}_{b}", [128, TC], F32)) for i in range(2)]
                xct = [BE(nc.sbuf_tensor(f"xct{i}_{b}", [128, TC], F32)) for i in range(2)]
                ztt = [BE(nc.sbuf_tensor(f"ztt{i}_{b}", [128, TC], F32)) for i in range(2)]
                u_t = BE(nc.sbuf_tensor(f"u_t_{b}", [128, TC], F32))
                dA = [BE(nc.sbuf_tensor(f"dA{i}_{b}", [128, TC], F32)) for i in range(3)]
                b_all = BE(nc.sbuf_tensor(f"b_all_{b}", [128, DS * TC], F32))
                h_all = BE(nc.sbuf_tensor(f"h_all_{b}", [128, DS * TC], F32))
                mc_all = BE(nc.sbuf_tensor(f"mc_all_{b}", [128, DS * TC], BF16))
                yt = [BE(nc.sbuf_tensor(f"yt{i}_{b}", [128, TC], F32)) for i in range(2)]
                ymix = BE(nc.sbuf_tensor(f"ymix_{b}", [128, NDC * TC], BF16))
                opw = [BE(nc.sbuf_tensor(f"opw{i}_{b}", [128, NDC * 128], BF16)) for i in range(2)]
                ps_y = [BE(nc.psum_tensor(f"ps_y{i}_{b}", [128, TC], F32)) for i in range(2)]
                ps_op = [BE(nc.psum_tensor(f"ps_op{i}_{b}", [128, TC], F32)) for i in range(2)]

                for tb in range(NTB):
                    t0 = tb * TC
                    P.wait("sync", ve)
                    for row in range(2 * DS):
                        dma(cache[:, row * TC:(row + 1) * TC],
                            s_bc[b, row:row + 1, t0:t0 + TC].partition_broadcast(128), lbc)
                    P.wait("vector", lbc)
                    tts_ve = {}
                    id_ev, yt_ev, ymix_ev, exps_ev, lact_ev, mc_ev = {}, {}, {}, {}, {}, {}

                    def emit_loads(dc):
                        if dc >= NDC:
                            return
                        io = dc % 2
                        if dc >= 2:
                            P.wait("sync", ve, ymix_ev[dc - 2])
                            P.wait("sync", se, exps_ev[dc - 2])
                        dma(dtt[io][:], s_dt[b, dc, :, t0:t0 + TC], lact2[io])
                        dma(xct[io][:], s_xc[b, dc, :, t0:t0 + TC], lact2[io])
                        dma(ztt[io][:], s_z[b, dc, :, t0:t0 + TC], lact2[io])
                        lact_ev[dc] = lact2[io].n

                    def emit_ytail(dc):
                        io = dc % 2
                        P.wait("vector", pe, id_ev[dc])
                        yt_ev[dc] = P.op("vector", lambda e, io=io, dc=dc: e.scalar_tensor_tensor(
                            out=yt[io][:], in0=xct[io][:], scalar=D_sb[:, dc:dc + 1],
                            in1=ps_y[dc % 2][:, 0:TC], op0=ALU.mult, op1=ALU.add), ve)
                        ymix_ev[dc] = P.op("vector", lambda e, io=io, dc=dc: e.tensor_mul(
                            out=ymix[:, dc * TC:(dc + 1) * TC], in0=yt[io][:],
                            in1=ztt[io][:]), ve)

                    emit_loads(0)
                    emit_loads(1)
                    for dc in range(NDC):
                        io = dc % 2
                        P.wait("vector", lact2[io], lact_ev[dc])
                        P.op("vector", lambda e, io=io: e.tensor_mul(
                            out=u_t[:], in0=dtt[io][:], in1=xct[io][:]), ve)
                        b_ev = P.op("vector", lambda e: e.tensor_tensor(
                            out=b_all[:].rearrange("p (s t) -> p s t", s=DS),
                            in0=u_t[:].unsqueeze(1).broadcast_to([128, DS, TC]),
                            in1=cache[:, 0:DS * TC].rearrange("p (s t) -> p s t", s=DS),
                            op=ALU.mult), ve)
                        if dc >= 1:
                            emit_ytail(dc - 1)
                            emit_loads(dc + 1)
                        P.wait("scalar", lact2[io], lact_ev[dc])
                        for si in range(DS):
                            da = dA[si % 3]
                            if si >= 3:
                                P.wait("scalar", ve, tts_ve[si - 3])
                            elif dc >= 1:
                                P.wait("scalar", ve, tts_ve[DS - 1])
                            exp_val = P.op("scalar", lambda e, io=io, dc=dc, si=si, da=da: e.activation(
                                out=da[:], in_=dtt[io][:], func=AF.Exp,
                                scale=A_sb[:, dc * DS + si: dc * DS + si + 1]), se)
                            P.wait("vector", se, exp_val)
                            if tb == 0:
                                tts_ve[si] = P.op("vector", lambda e, si=si, da=da: e.tensor_tensor_scan(
                                    out=h_all[:, si * TC:(si + 1) * TC], data0=da[:],
                                    data1=b_all[:, si * TC:(si + 1) * TC],
                                    initial=0.0, op0=ALU.mult, op1=ALU.add), ve)
                            else:
                                tts_ve[si] = P.op("vector", lambda e, dc=dc, si=si, da=da: e.tensor_tensor_scan(
                                    out=h_all[:, si * TC:(si + 1) * TC], data0=da[:],
                                    data1=b_all[:, si * TC:(si + 1) * TC],
                                    initial=state_sb[:, dc * DS + si: dc * DS + si + 1],
                                    op0=ALU.mult, op1=ALU.add), ve)
                        exps_ev[dc] = se.n
                        if tb < NTB - 1:
                            P.op("vector", lambda e, dc=dc: e.tensor_copy(
                                out=state_sb[:, dc * DS:(dc + 1) * DS].unsqueeze(2),
                                in_=h_all[:].rearrange("p (s t) -> p s t", s=DS)[:, :, TC - 1:TC]), ve)
                        mc_ev[dc] = P.op("vector", lambda e: e.tensor_tensor(
                            out=mc_all[:].rearrange("p (s t) -> p s t", s=DS),
                            in0=h_all[:].rearrange("p (s t) -> p s t", s=DS),
                            in1=cache[:, DS * TC:2 * DS * TC].rearrange("p (s t) -> p s t", s=DS),
                            op=ALU.mult), ve)
                        py = ps_y[dc % 2]
                        P.wait("tensor", ve, mc_ev[dc])
                        if dc >= 2:
                            P.wait("tensor", ve, yt_ev[dc - 2])
                        for si in range(DS):
                            id_ev[dc] = P.op("tensor", lambda e, si=si, py=py: e.matmul(
                                out=py[:, 0:TC], lhsT=id_bf[:],
                                rhs=mc_all[:, si * TC:(si + 1) * TC],
                                start=(si == 0), stop=(si == DS - 1)), pe)
                    emit_ytail(NDC - 1)
                    opmm_ev, mixcp_ev = {}, {}
                    lw2_ev = {}
                    lw2_ev[0] = dma(opw[0][:].rearrange("p (c n) -> p c n", n=128),
                                    s_opbf[:, 0:128].rearrange("(c p) n -> p c n", p=128), lw2[0])
                    for m in range(KC):
                        ms = m % 2
                        po = ps_op[ms]
                        if m + 1 < KC:
                            if m >= 1:
                                P.wait("sync", pe, opmm_ev[m - 1])
                            lw2_ev[(m + 1) % 2] = dma(
                                opw[(m + 1) % 2][:].rearrange("p (c n) -> p c n", n=128),
                                s_opbf[:, (m + 1) * 128:(m + 2) * 128].rearrange(
                                    "(c p) n -> p c n", p=128), lw2[(m + 1) % 2])
                        P.wait("tensor", lw2[ms], lw2_ev[ms])
                        if m < 2:
                            P.wait("tensor", ve, ymix_ev[NDC - 1])
                        else:
                            P.wait("tensor", ve, mixcp_ev[m - 2])
                        for dc in range(NDC):
                            opmm_ev[m] = P.op("tensor", lambda e, m=m, dc=dc, po=po, ms=ms: e.matmul(
                                out=po[:, 0:TC], lhsT=opw[ms][:, dc * 128:(dc + 1) * 128],
                                rhs=ymix[:, dc * TC:(dc + 1) * TC],
                                start=(dc == 0), stop=(dc == NDC - 1)), pe)
                        P.wait("vector", pe, opmm_ev[m])
                        P.wait("vector", stb2[ms])
                        mt = yt[ms]
                        mixcp_ev[m] = P.op("vector", lambda e, po=po, mt=mt: e.tensor_copy(
                            out=mt[:], in_=po[:, 0:TC]), ve)
                        P.wait("sync", ve, mixcp_ev[m])
                        store(s_mix[b, m, :, t0:t0 + TC], mt[:], stb2[ms])
                P.barrier(ALLSEMS)

        def phase_C(b):
            with ExitStack() as cctx:
                CE = cctx.enter_context
                x_fm = CE(nc.sbuf_tensor(f"xc_fm_{b}", [128, KC * TC], F32))
                xtm = [CE(nc.sbuf_tensor(f"xtmc{i}_{b}", [128, D], F32)) for i in range(2)]
                mix_fm = CE(nc.sbuf_tensor(f"mix_fm_{b}", [128, KC * TC], F32))
                x2_bf = CE(nc.sbuf_tensor(f"x2_bf_{b}", [128, KC * TC], BF16))
                sq_bf = CE(nc.sbuf_tensor(f"sqc_bf_{b}", [128, KC * TC], BF16))
                rrow = CE(nc.sbuf_tensor(f"rrow_c_{b}", [1, 2 * TC], F32))
                trow = CE(nc.sbuf_tensor(f"trow_c_{b}", [1, TC], F32))
                ht_all = CE(nc.sbuf_tensor(f"ht_all_{b}", [128, KC * 2 * TC], BF16))
                lg_sb = CE(nc.sbuf_tensor(f"lg_sb_{b}", [8, TC], F32))
                lgtm = CE(nc.sbuf_tensor(f"lgtm_{b}", [128, NTT * 8], F32))
                tk = CE(nc.sbuf_tensor(f"tk_{b}", [128, NTT * 20], F32))
                gtm = CE(nc.sbuf_tensor(f"gtm_{b}", [128, NTT * 8], F32))
                gfm = CE(nc.sbuf_tensor(f"gfm_{b}", [8, TC], F32))
                gfm_bf = CE(nc.sbuf_tensor(f"gfm_bf_{b}", [8, TC], BF16))
                wm = [CE(nc.sbuf_tensor(f"wmc{i}_{b}", [128, EC * 128], BF16)) for i in range(2)]
                b1row = CE(nc.sbuf_tensor(f"b1row_{b}", [1, EDIM], BF16))
                ones_tc = CE(nc.sbuf_tensor(f"ones_tc_{b}", [1, TC], BF16))
                ghid = CE(nc.sbuf_tensor(f"ghid_{b}", [128, 2 * EC * TC], BF16))
                sqg = [CE(nc.sbuf_tensor(f"sqg{i}_{b}", [128, TC], F32)) for i in range(2)]
                ug = [CE(nc.sbuf_tensor(f"ug{i}_{b}", [128, TC], F32)) for i in range(2)]
                gbc = [CE(nc.sbuf_tensor(f"gbc{i}_{b}", [128, TC], F32)) for i in range(2)]
                moe_sb = CE(nc.sbuf_tensor(f"moe_sb_{b}", [128, 2 * KC * TC], F32))
                ft = [CE(nc.sbuf_tensor(f"ftc{i}_{b}", [128, TC], F32)) for i in range(2)]
                f_sb = CE(nc.sbuf_tensor(f"f_sb_{b}", [128, KC * TC], F32))
                ps_mm = [CE(nc.psum_tensor(f"ps_cm{i}_{b}", [128, TC], F32)) for i in range(2)]
                ps_st = CE(nc.psum_tensor(f"ps_cst_{b}", [128, 2 * TC], F32))
                ps_tr = [CE(nc.psum_tensor(f"ps_trc{i}_{b}", [128, 128], F32)) for i in range(2)]
                ps_lg = CE(nc.psum_tensor(f"ps_lg_{b}", [8, TC], F32))
                ps_b2 = CE(nc.psum_tensor(f"ps_b2_{b}", [128, TC], F32))

                P.op("vector", lambda e: e.memset(ones_tc[:], 1.0), ve)
                for th in range(2):
                    tbs = [2 * th, 2 * th + 1]
                    for j, tb in enumerate(tbs):
                        t0 = tb * TC
                        emit_x_load(b, tb, x_fm, xtm, ps_tr)
                        P.wait("sync", ve)
                        for kc in range(KC):
                            dma(mix_fm[:, kc * TC:(kc + 1) * TC], s_mix[b, kc, :, t0:t0 + TC], lmix)
                        P.wait("vector", lmix)
                        P.wait("vector", stb)
                        for kc in range(KC):
                            P.op("vector", lambda e, kc=kc: e.tensor_add(
                                out=x_fm[:, kc * TC:(kc + 1) * TC],
                                in0=x_fm[:, kc * TC:(kc + 1) * TC],
                                in1=mix_fm[:, kc * TC:(kc + 1) * TC]), ve)
                        emit_rmsnorm(x_fm, n2w,
                                     lambda kc, j=j: ht_all[:, (kc * 2 + j) * TC:(kc * 2 + j + 1) * TC],
                                     sq_bf, ps_st, ps_mm[0], rrow)
                        P.wait("tensor", ve)
                        for kc in range(KC):
                            P.op("tensor", lambda e, kc=kc, j=j: e.matmul(
                                out=ps_lg[:, 0:TC], lhsT=rww_bf[:, kc * 8:(kc + 1) * 8],
                                rhs=ht_all[:, (kc * 2 + j) * TC:(kc * 2 + j + 1) * TC],
                                start=(kc == 0), stop=(kc == KC - 1)), pe)
                        P.wait("vector", pe)
                        P.op("vector", lambda e: e.tensor_copy(out=lg_sb[:], in_=ps_lg[0:8, 0:TC]), ve)
                        emit_logit_transpose(lg_sb, lgtm, ps_tr)
                        emit_topk_gates(lgtm, gtm, tk, se)
                        P.wait("vector", stc)
                        emit_rows_back(gtm, gfm, 8, ps_tr)
                        P.wait("sync", ve)
                        store(s_gates[b, tb], gfm[:], stc)
                    P.op("vector", lambda e: e.memset(moe_sb[:], 0.0), ve)
                    w1v = s_w1bf.rearrange("e (c p) m -> e p c m", p=128)
                    w2v = s_w2bf.rearrange("e (c p) m -> e p c m", p=128)
                    wm_user = {0: 0, 1: 0}
                    b1_user = 0
                    for e_i in range(NE):
                        P.wait("gpsimd", pe, b1_user)
                        dma(b1row[:], w["b1"][e_i:e_i + 1, :], gd, q="gpsimd")
                        b1_gd = gd.n
                        gh_ev = {}
                        ug_ev = {}
                        for hc in range(EC):
                            slot = hc % 2
                            P.wait("sync", pe, wm_user[slot])
                            wld = dma(wm[slot][:, 0:KC * 128].rearrange("p (c n) -> p c n", n=128),
                                      w1v[e_i][:, :, hc * 128:(hc + 1) * 128], lw2[slot])
                            for j, tb in enumerate(tbs):
                                pb = ps_mm[j]
                                P.wait("tensor", lw2[slot], wld)
                                if hc == 0 and j == 0:
                                    P.wait("tensor", gd, b1_gd)
                                if hc >= 1:
                                    P.wait("tensor", ve, gh_ev[(hc - 1, j)])
                                for kc in range(KC):
                                    P.op("tensor", lambda e, hc=hc, kc=kc, j=j, slot=slot, pb=pb: e.matmul(
                                        out=pb[:, 0:TC],
                                        lhsT=wm[slot][:, kc * 128:(kc + 1) * 128],
                                        rhs=ht_all[:, (kc * 2 + j) * TC:(kc * 2 + j + 1) * TC],
                                        start=(kc == 0), stop=False), pe)
                                mm_ev = P.op("tensor", lambda e, hc=hc, pb=pb: e.matmul(
                                    out=pb[:, 0:TC], lhsT=b1row[0:1, hc * 128:(hc + 1) * 128],
                                    rhs=ones_tc[0:1, 0:TC], start=False, stop=True), pe)
                                wm_user[slot] = mm_ev
                                b1_user = mm_ev
                                P.wait("scalar", pe, mm_ev)
                                if hc >= 1:
                                    P.wait("scalar", ve, ug_ev[(hc - 1, j)])
                                sqv = P.op("scalar", lambda e, pb=pb, j=j: e.activation(
                                    out=sqg[j][:], in_=pb[:, 0:TC], func=AF.Square,
                                    scale=SQRT_GK2), se)
                                P.wait("vector", se, sqv)
                                ug_ev[(hc, j)] = P.op("vector", lambda e, pb=pb, j=j: e.scalar_tensor_tensor(
                                    out=ug[j][:], in0=sqg[j][:], scalar=GK1, in1=pb[:, 0:TC],
                                    op0=ALU.add, op1=ALU.mult), ve)
                                P.wait("scalar", ve, ug_ev[(hc, j)])
                                sgv = P.op("scalar", lambda e, j=j: e.activation(
                                    out=ug[j][:], in_=ug[j][:], func=AF.Sigmoid), se)
                                P.wait("vector", se, sgv)
                                gh_ev[(hc, j)] = P.op("vector", lambda e, j=j, hc=hc, pb=pb: e.tensor_mul(
                                    out=ghid[:, (j * EC + hc) * TC:(j * EC + hc + 1) * TC],
                                    in0=ug[j][:], in1=pb[:, 0:TC]), ve)
                        P.wait("sync", ve)
                        P.wait("sync", stc)
                        for j, tb in enumerate(tbs):
                            dma(gbc[j][:], s_gates[b, tb, e_i:e_i + 1, :].partition_broadcast(128), lbc)
                        gbc_ev = lbc.n
                        madd_ev = {}
                        for m in range(KC):
                            slot = m % 2
                            P.wait("sync", pe, wm_user[slot])
                            wld = dma(wm[slot][:, 0:EC * 128].rearrange("p (c n) -> p c n", n=128),
                                      w2v[e_i][:, :, m * 128:(m + 1) * 128], lw2[slot])
                            for j, tb in enumerate(tbs):
                                pb = ps_mm[j]
                                P.wait("tensor", lw2[slot], wld)
                                if m == 0:
                                    P.wait("tensor", ve, gh_ev[(EC - 1, j)])
                                else:
                                    P.wait("tensor", ve, madd_ev[(m - 1, j)])
                                mm_ev = None
                                for hc in range(EC):
                                    mm_ev = P.op("tensor", lambda e, m=m, hc=hc, j=j, slot=slot, pb=pb: e.matmul(
                                        out=pb[:, 0:TC],
                                        lhsT=wm[slot][:, hc * 128:(hc + 1) * 128],
                                        rhs=ghid[:, (j * EC + hc) * TC:(j * EC + hc + 1) * TC],
                                        start=(hc == 0), stop=(hc == EC - 1)), pe)
                                wm_user[slot] = mm_ev
                                P.wait("vector", pe, mm_ev)
                                P.wait("vector", lbc, gbc_ev)
                                P.op("vector", lambda e, j=j, pb=pb: e.tensor_mul(
                                    out=ft[j][:], in0=pb[:, 0:TC], in1=gbc[j][:]), ve)
                                mv = moe_sb[:, (j * KC + m) * TC:(j * KC + m + 1) * TC]
                                madd_ev[(m, j)] = P.op("vector", lambda e, mv=mv, j=j: e.tensor_add(
                                    out=mv, in0=mv, in1=ft[j][:]), ve)
                    for j, tb in enumerate(tbs):
                        t0 = tb * TC
                        emit_x_load(b, tb, x_fm, xtm, ps_tr)
                        P.wait("sync", ve)
                        for kc in range(KC):
                            dma(mix_fm[:, kc * TC:(kc + 1) * TC], s_mix[b, kc, :, t0:t0 + TC], lmix)
                        P.wait("sync", stc)
                        dma(gfm[:], s_gates[b, tb], lbc)
                        P.wait("vector", lbc)
                        P.op("vector", lambda e: e.tensor_copy(out=gfm_bf[:], in_=gfm[:]), ve)
                        P.wait("tensor", ve)
                        for m in range(KC):
                            P.op("tensor", lambda e, m=m: e.matmul(
                                out=ps_b2[:, 0:TC], lhsT=b2_bf[:, m * 128:(m + 1) * 128],
                                rhs=gfm_bf[:], start=True, stop=True), pe)
                            P.wait("vector", pe)
                            mv = moe_sb[:, (j * KC + m) * TC:(j * KC + m + 1) * TC]
                            P.op("vector", lambda e, mv=mv: e.tensor_add(
                                out=mv, in0=mv, in1=ps_b2[:, 0:TC]), ve)
                            P.wait("tensor", ve)
                            P.op("vector", lambda e, m=m: e.scalar_tensor_tensor(
                                out=x_fm[:, m * TC:(m + 1) * TC],
                                in0=x_fm[:, m * TC:(m + 1) * TC], scalar=2.0,
                                in1=mix_fm[:, m * TC:(m + 1) * TC],
                                op0=ALU.mult, op1=ALU.add), ve)
                            P.op("vector", lambda e, m=m, mv=mv: e.tensor_add(
                                out=x_fm[:, m * TC:(m + 1) * TC],
                                in0=x_fm[:, m * TC:(m + 1) * TC], in1=mv), ve)
                        emit_layernorm(x_fm, mix_fm, x2_bf, sq_bf, ps_st, ps_mm, rrow, trow)
                        P.op("vector", lambda e: e.tensor_copy(out=x2_bf[:], in_=mix_fm[:]), ve)
                        P.wait("tensor", ve)
                        for kc in range(KC):
                            P.op("tensor", lambda e, kc=kc: e.matmul(
                                out=ps_lg[:, 0:TC], lhsT=gww_bf[:, kc * 8:(kc + 1) * 8],
                                rhs=x2_bf[:, kc * TC:(kc + 1) * TC],
                                start=(kc == 0), stop=(kc == KC - 1)), pe)
                        P.wait("vector", pe)
                        P.op("vector", lambda e: e.tensor_copy(out=lg_sb[:], in_=ps_lg[0:8, 0:TC]), ve)
                        emit_logit_transpose(lg_sb, lgtm, ps_tr)
                        col = lambda tt, i: tk[:, tt * 20 + i: tt * 20 + i + 1]
                        lgm_ = lambda tt: tk[:, tt * 20 + 9: tt * 20 + 17]
                        lg8_ = lambda tt: lgtm[:, tt * 8:(tt + 1) * 8]
                        g8_ = lambda tt: gtm[:, tt * 8:(tt + 1) * 8]
                        for tt in range(NTT):
                            P.op("vector", lambda e, tt=tt: e.reduce_max(
                                out=col(tt, 0), in_=lg8_(tt), axis=AX), ve)
                        for tt in range(NTT):
                            P.op("vector", lambda e, tt=tt: e.tensor_scalar(
                                out=g8_(tt), in0=lg8_(tt), scalar1=col(tt, 0),
                                scalar2=None, op0=ALU.is_equal), ve)
                        for tt in range(NTT):
                            P.op("vector", lambda e, tt=tt: e.scalar_tensor_tensor(
                                out=lgm_(tt), in0=g8_(tt), scalar=-1e30, in1=lg8_(tt),
                                op0=ALU.mult, op1=ALU.add), ve)
                        for tt in range(NTT):
                            P.op("vector", lambda e, tt=tt: e.reduce_max(
                                out=col(tt, 1), in_=lgm_(tt), axis=AX), ve)
                        for tt in range(NTT):
                            P.op("vector", lambda e, tt=tt: e.tensor_sub(
                                out=col(tt, 2), in0=col(tt, 1), in1=col(tt, 0)), ve)
                        for tt in range(NTT):
                            P.op("vector", lambda e, tt=tt: e.tensor_copy(
                                out=col(tt, 7), in_=g8_(tt)[:, 0:1]), ve)
                        for tt in range(NTT):
                            P.op("vector", lambda e, tt=tt: e.tensor_scalar(
                                out=col(tt, 8), in0=lgm_(tt)[:, 1:2], scalar1=col(tt, 1),
                                scalar2=None, op0=ALU.is_equal), ve)
                        for tt in range(NTT):
                            nmv = P.op("vector", lambda e, tt=tt: e.tensor_scalar_mul(
                                out=col(tt, 3), in0=col(tt, 0), scalar1=-1.0), ve)
                        P.wait("scalar", ve, nmv)
                        for tt in range(NTT):
                            P.op("scalar", lambda e, tt=tt: e.activation(
                                out=lgm_(tt), in_=lg8_(tt), func=AF.Exp,
                                bias=col(tt, 3)), se)
                        for tt in range(NTT):
                            ex2 = P.op("scalar", lambda e, tt=tt: e.activation(
                                out=col(tt, 4), in_=col(tt, 2), func=AF.Exp), se)
                        P.wait("vector", se, ex2)
                        for tt in range(NTT):
                            zv = P.op("vector", lambda e, tt=tt: e.reduce_sum(
                                out=col(tt, 5), in_=lgm_(tt), axis=AX), ve)
                        P.wait("scalar", ve, zv)
                        for tt in range(NTT):
                            P.op("scalar", lambda e, tt=tt: e.activation(
                                out=col(tt, 5), in_=col(tt, 5), func=AF.Ln), se)
                        for tt in range(NTT):
                            rzv = P.op("scalar", lambda e, tt=tt: e.activation(
                                out=col(tt, 6), in_=col(tt, 5), func=AF.Exp,
                                scale=-1.0), se)
                        P.wait("vector", se, rzv)
                        for tt in range(NTT):
                            P.op("vector", lambda e, tt=tt: e.tensor_mul(
                                out=col(tt, 7), in0=col(tt, 7), in1=col(tt, 6)), ve)
                        for tt in range(NTT):
                            P.op("vector", lambda e, tt=tt: e.tensor_mul(
                                out=col(tt, 8), in0=col(tt, 8), in1=col(tt, 4)), ve)
                        for tt in range(NTT):
                            P.op("vector", lambda e, tt=tt: e.tensor_mul(
                                out=col(tt, 8), in0=col(tt, 8), in1=col(tt, 6)), ve)
                        for tt in range(NTT):
                            P.op("vector", lambda e, tt=tt: e.tensor_copy(
                                out=gtm[:, tt * 8 + 0: tt * 8 + 1], in_=col(tt, 7)), ve)
                        for tt in range(NTT):
                            P.op("vector", lambda e, tt=tt: e.tensor_copy(
                                out=gtm[:, tt * 8 + 1: tt * 8 + 2], in_=col(tt, 8)), ve)
                        P.wait("vector", stc)
                        emit_rows_back(gtm, gfm, 2, ps_tr)
                        P.wait("sync", ve)
                        store(s_coef[b, tb], gfm[0:2, :], stc)
                        P.wait("sync", stc)
                        P.wait("sync", ve)
                        dma(gbc[0][:], s_coef[b, tb, 0:1, :].partition_broadcast(128), lbc)
                        dma(gbc[1][:], s_coef[b, tb, 1:2, :].partition_broadcast(128), lbc)
                        cf_ev = lbc.n
                        wm_user2 = {0: pe.n, 1: pe.n}
                        sil_ev, cmb_ev = {}, {}
                        for ei, (sw, bsb) in enumerate(((s_w0t, fb0), (s_w1t, fb1))):
                            for m in range(KC):
                                i = ei * KC + m
                                slot = i % 2
                                P.wait("sync", pe, wm_user2[slot])
                                wld = dma(wm[slot][:, 0:KC * 128].rearrange("p (k n) -> p k n", k=KC),
                                          sw[:, :, m * 128:(m + 1) * 128].rearrange("k p n -> p k n"), lw2[slot])
                                pb = ps_mm[slot]
                                P.wait("tensor", lw2[slot], wld)
                                if i >= 2:
                                    P.wait("tensor", se, sil_ev[i - 2])
                                    P.wait("tensor", ve, cmb_ev[i - 2])
                                mm_ev = None
                                for kc in range(KC):
                                    mm_ev = P.op("tensor", lambda e, m=m, kc=kc, slot=slot, pb=pb: e.matmul(
                                        out=pb[:, 0:TC],
                                        lhsT=wm[slot][:, kc * 128:(kc + 1) * 128],
                                        rhs=x2_bf[:, kc * TC:(kc + 1) * TC],
                                        start=(kc == 0), stop=(kc == KC - 1)), pe)
                                wm_user2[slot] = mm_ev
                                P.wait("scalar", pe, mm_ev)
                                if i >= 2:
                                    P.wait("scalar", ve, cmb_ev[i - 2])
                                et = ft[slot]
                                sil_ev[i] = P.op("scalar", lambda e, m=m, pb=pb, et=et, bsb=bsb: e.activation(
                                    out=et[:], in_=pb[:, 0:TC],
                                    func=AF.Silu, bias=bsb[:, m:m + 1]), se)
                                P.wait("vector", se, sil_ev[i])
                                if i == 0:
                                    P.wait("vector", lbc, cf_ev)
                                if ei == 0:
                                    cmb_ev[i] = P.op("vector", lambda e, m=m, et=et: e.tensor_mul(
                                        out=f_sb[:, m * TC:(m + 1) * TC], in0=et[:],
                                        in1=gbc[0][:]), ve)
                                else:
                                    P.op("vector", lambda e, m=m, et=et: e.tensor_mul(
                                        out=et[:], in0=et[:], in1=gbc[1][:]), ve)
                                    cmb_ev[i] = P.op("vector", lambda e, m=m, et=et: e.tensor_add(
                                        out=f_sb[:, m * TC:(m + 1) * TC],
                                        in0=f_sb[:, m * TC:(m + 1) * TC], in1=et[:]), ve)
                        for m in range(KC):
                            P.op("vector", lambda e, m=m: e.tensor_add(
                                out=f_sb[:, m * TC:(m + 1) * TC],
                                in0=f_sb[:, m * TC:(m + 1) * TC],
                                in1=mix_fm[:, m * TC:(m + 1) * TC]), ve)
                        emit_layernorm(f_sb, x_fm, x2_bf, sq_bf, ps_st, ps_mm, rrow, trow)
                        P.wait("tensor", ve)
                        ocp_ev = {}
                        for tt in range(NTT):
                            P.wait("vector", stc)
                            for kc in range(KC):
                                i = tt * KC + kc
                                if i >= 2:
                                    P.wait("tensor", ve, ocp_ev[i - 2])
                                tr = P.op("tensor", lambda e, tt=tt, kc=kc, i=i: e.transpose(
                                    out=ps_tr[i % 2][:],
                                    in_=x_fm[:, kc * TC + tt * 128: kc * TC + (tt + 1) * 128],
                                    identity=id_f[:]), pe)
                                P.wait("vector", pe, tr)
                                ocp_ev[i] = P.op("vector", lambda e, kc=kc, i=i: e.tensor_copy(
                                    out=xtm[1][:, kc * 128:(kc + 1) * 128], in_=ps_tr[i % 2][:]), ve)
                            P.wait("sync", ve)
                            store(out[b, t0 + tt * 128: t0 + (tt + 1) * 128, :], xtm[1][:], stc)
                P.barrier(ALLSEMS)

        for b in range(NB):
            if "A" in phases:
                phase_A(b)
            if "B" in phases:
                phase_B(b)
            if "C" in phases:
                phase_C(b)

        for sm in ALLSEMS:
            P.wait("sync", sm)

        with nc.Block() as block:
            @block.sync
            def _(eng):
                for fn in P.streams["sync"]:
                    fn(eng)

            @block.vector
            def _(eng):
                for fn in P.streams["vector"]:
                    fn(eng)

            @block.scalar
            def _(eng):
                for fn in P.streams["scalar"]:
                    fn(eng)

            @block.tensor
            def _(eng):
                for fn in P.streams["tensor"]:
                    fn(eng)

            @block.gpsimd
            def _(eng):
                for fn in P.streams["gpsimd"]:
                    fn(eng)

    return nc


def kernel(**inputs):
    ncores = 8
    import ml_dtypes
    x_full = np.ascontiguousarray(np.asarray(inputs["L0_attn_x"], dtype=np.float32))
    base = {
        "norm1_w": inputs["L0_attn_norm1_w"], "norm2_w": inputs["L0_attn_norm2_w"],
        "in_proj_w": inputs["L0_attn_in_proj_w"], "conv_w": inputs["L0_attn_conv_w"],
        "conv_b": inputs["L0_attn_conv_b"], "x_proj_w": inputs["L0_attn_x_proj_w"],
        "dt_proj_w": inputs["L0_attn_dt_proj_w"], "dt_proj_b": inputs["L0_attn_dt_proj_b"],
        "A_log": inputs["L0_attn_A_log"], "D_param": inputs["L0_attn_D_param"],
        "out_proj_w": inputs["L0_attn_out_proj_w"], "router_w": inputs["L0_attn_router_w"],
        "w1": inputs["L0_attn_w1"], "b1": inputs["L0_attn_b1"],
        "w2": inputs["L0_attn_w2"], "b2": inputs["L0_attn_b2"],
        "gate_w": inputs["L0_ffn_gate_w"], "ffn_w0": inputs["L0_ffn_w0"],
        "ffn_b0": inputs["L0_ffn_b0"], "ffn_w1": inputs["L0_ffn_w1"],
        "ffn_b1": inputs["L0_ffn_b1"],
        "ident_bf": np.eye(128, dtype=ml_dtypes.bfloat16),
        "ident_f32": np.eye(128, dtype=np.float32),
    }
    base = {k: np.ascontiguousarray(np.asarray(v, dtype=(ml_dtypes.bfloat16 if k == "ident_bf" else np.float32)))
            for k, v in base.items()}
    nc = build_program()
    in_maps = []
    for c in range(ncores):
        m = dict(base)
        m["x"] = np.ascontiguousarray(x_full[c * NB:(c + 1) * NB])
        in_maps.append(m)
    res = run_bass_kernel_spmd(nc, in_maps, core_ids=list(range(ncores)))
    return np.concatenate([np.asarray(r["out"], np.float32) for r in res.results], axis=0)


# version 03
# speedup vs baseline: 1.7497x; 1.0390x over previous
import math
import numpy as np
import concourse.bass as bass
import concourse.mybir as mybir
from concourse.bass_utils import run_bass_kernel_spmd
from contextlib import ExitStack

F32 = mybir.dt.float32
BF16 = mybir.dt.bfloat16
AF = mybir.ActivationFunctionType
ALU = mybir.AluOpType
AX = mybir.AxisListType.X

NB = 4
L = 2048
D = 1024
DI = 2048
DS = 16
NE = 8
EDIM = 2048
TC = 512
NTB = L // TC
KC = D // 128
NDC = DI // 128
EC = EDIM // 128
NTT = TC // 128

GK1 = 2.0 * 0.7978845608028654
GK2 = GK1 * 0.044715
SQRT_GK2 = math.sqrt(GK2)

ENGINES = ("sync", "vector", "scalar", "tensor", "gpsimd")


class Sem:
    def __init__(self, stack, nc, name):
        self.h = stack.enter_context(nc.semaphore(name))
        self.n = 0


class Prog:
    def __init__(self):
        self.streams = {e: [] for e in ENGINES}

    def op(self, eng, fn, sem, k=1):
        sem.n += k

        def wrapped(e, fn=fn, sem=sem, k=k):
            fn(e).then_inc(sem.h, k)
        self.streams[eng].append(wrapped)
        return sem.n

    def wait(self, eng, sem, val=None):
        v = sem.n if val is None else val
        if v <= 0:
            return
        self.streams[eng].append(lambda e, s=sem, v=v: e.wait_ge(s.h, v))

    def barrier(self, sems):
        for eng in ENGINES:
            for s in sems:
                self.wait(eng, s)


def build_program(dbg=False, phases="ABC"):
    nc = bass.Bass("TRN2", target_bir_lowering=False)

    x_in = nc.dram_tensor("x", [NB, L, D], F32, kind="ExternalInput").ap()
    out = nc.dram_tensor("out", [NB, L, D], F32, kind="ExternalOutput").ap()

    w = {}
    for name, shape in [
        ("norm1_w", [D]), ("norm2_w", [D]), ("in_proj_w", [D, 2 * DI]),
        ("conv_w", [DI, 4]), ("conv_b", [DI]), ("x_proj_w", [DI, 96]),
        ("dt_proj_w", [64, DI]), ("dt_proj_b", [DI]), ("A_log", [DI, DS]),
        ("D_param", [DI]), ("out_proj_w", [DI, D]), ("router_w", [D, NE]),
        ("w1", [NE, D, EDIM]), ("b1", [NE, EDIM]), ("w2", [NE, EDIM, D]),
        ("b2", [NE, D]), ("gate_w", [D, NE]), ("ffn_w0", [D, D]),
        ("ffn_b0", [D]), ("ffn_w1", [D, D]), ("ffn_b1", [D]),
        ("ident_bf", [128, 128]), ("ident_f32", [128, 128]),
    ]:
        w[name] = nc.dram_tensor(name, shape, BF16 if name == "ident_bf" else F32,
                                 kind="ExternalInput").ap()

    skind = "ExternalOutput" if dbg else "Internal"
    s_xc = nc.dram_tensor("s_xc", [NB, NDC, 128, L], F32, kind=skind).ap()
    s_z = nc.dram_tensor("s_z", [NB, NDC, 128, L], F32, kind=skind).ap()
    s_dt = nc.dram_tensor("s_dt", [NB, NDC, 128, L], F32, kind=skind).ap()
    s_bc = nc.dram_tensor("s_bc", [NB, 2 * DS, L], F32, kind=skind).ap()
    s_mix = nc.dram_tensor("s_mix", [NB, KC, 128, L], F32, kind=skind).ap()
    s_gates = nc.dram_tensor("s_gates", [NB, NTB, NE, TC], F32, kind=skind).ap()
    s_coef = nc.dram_tensor("s_coef", [NB, NTB, 2, TC], F32, kind=skind).ap()
    s_ipbf = nc.dram_tensor("s_ipbf", [D, 2 * DI], BF16, kind="Internal").ap()
    s_opbf = nc.dram_tensor("s_opbf", [DI, D], BF16, kind="Internal").ap()
    s_w1bf = nc.dram_tensor("s_w1bf", [NE, D, EDIM], BF16, kind="Internal").ap()
    s_w2bf = nc.dram_tensor("s_w2bf", [NE, EDIM, D], BF16, kind="Internal").ap()
    s_w0t = nc.dram_tensor("s_w0t", [KC, 128, D], BF16, kind="Internal").ap()
    s_w1t = nc.dram_tensor("s_w1t", [KC, 128, D], BF16, kind="Internal").ap()

    P = Prog()

    with ExitStack() as ctx:
        E = ctx.enter_context
        lset = Sem(ctx, nc, "lset")
        gd = Sem(ctx, nc, "gd")
        lx = Sem(ctx, nc, "lx")
        lw = Sem(ctx, nc, "lw")
        lact = Sem(ctx, nc, "lact")
        lbc = Sem(ctx, nc, "lbc")
        lmix = Sem(ctx, nc, "lmix")
        lwt = Sem(ctx, nc, "lwt")
        sta = Sem(ctx, nc, "sta")
        lw2 = [Sem(ctx, nc, f"lw2_{i}") for i in range(2)]
        lact2 = [Sem(ctx, nc, f"lact2_{i}") for i in range(2)]
        stz2 = [Sem(ctx, nc, f"stz2_{i}") for i in range(2)]
        stx2 = [Sem(ctx, nc, f"stx2_{i}") for i in range(2)]
        std2 = [Sem(ctx, nc, f"std2_{i}") for i in range(2)]
        stb2 = [Sem(ctx, nc, f"stb2_{i}") for i in range(2)]
        stb = Sem(ctx, nc, "stb")
        stc = Sem(ctx, nc, "stc")
        pe = Sem(ctx, nc, "pe")
        ve = Sem(ctx, nc, "ve")
        se = Sem(ctx, nc, "se")
        ALLSEMS = ([lset, gd, lx, lw, lact, lbc, lmix, lwt, sta, stb, stc, pe, ve, se]
                   + lw2 + lact2 + stz2 + stx2 + std2 + stb2)

        def dma(dst, src, sem, q="sync", slow=False):
            return P.op(q, lambda e, d=dst, s=src, sl=slow: e.dma_start(
                out=d, in_=s, allow_slow_non_contiguous=sl), sem, 16)

        def store(dst, src, sem):
            return P.op("sync", lambda e, d=dst, s=src: e.dma_start(out=d, in_=s), sem, 16)

        ones_col_bf = E(nc.sbuf_tensor("ones_col_bf", [128, 1], BF16))
        ones_row_f = E(nc.sbuf_tensor("ones_row_f", [1, 128], F32))
        id_bf = E(nc.sbuf_tensor("id_bf", [128, 128], BF16))
        id_f = E(nc.sbuf_tensor("id_f", [128, 128], F32))
        n1w = E(nc.sbuf_tensor("n1w", [128, KC], F32))
        n2w = E(nc.sbuf_tensor("n2w", [128, KC], F32))
        convw = E(nc.sbuf_tensor("convw", [128, NDC * 4], F32))
        convb = E(nc.sbuf_tensor("convb", [128, NDC], F32))
        dtb = E(nc.sbuf_tensor("dtb", [128, NDC], F32))
        A_sb = E(nc.sbuf_tensor("A_sb", [128, NDC * DS], F32))
        D_sb = E(nc.sbuf_tensor("D_sb", [128, NDC], F32))
        b1_sb = E(nc.sbuf_tensor("b1_sb", [128, NE * EC], F32))
        b2_bf = E(nc.sbuf_tensor("b2_bf", [NE, D], BF16))
        fb0 = E(nc.sbuf_tensor("fb0", [128, KC], F32))
        fb1 = E(nc.sbuf_tensor("fb1", [128, KC], F32))
        xpw_bf = E(nc.sbuf_tensor("xpw_bf", [128, NDC * 96], BF16))
        dtw_bf = E(nc.sbuf_tensor("dtw_bf", [64, DI], BF16))
        rww_bf = E(nc.sbuf_tensor("rww_bf", [128, KC * NE], BF16))
        gww_bf = E(nc.sbuf_tensor("gww_bf", [128, KC * NE], BF16))
        state_sb = E(nc.sbuf_tensor("state_sb", [128, NDC * DS], F32))
        hist_bf = E(nc.sbuf_tensor("hist_bf", [128, NDC * 3], BF16))

        const_sb = E(nc.sbuf_tensor("const_sb", [128, 4], F32))
        for col, val in ((0, 1e-6), (1, 1e-5), (2, 1.0)):
            P.op("vector", lambda e, col=col, val=val: e.memset(const_sb[:, col:col + 1], val), ve)
            nc.const_aps.register(F32, val, const_sb[:, col:col + 1])

        dma(n1w[:], w["norm1_w"].rearrange("(c p) -> p c", p=128), lset, slow=True)
        dma(n2w[:], w["norm2_w"].rearrange("(c p) -> p c", p=128), lset, slow=True)
        dma(convw[:].rearrange("p (c k) -> p c k", k=4), w["conv_w"].rearrange("(c p) k -> p c k", p=128), lset)
        dma(convb[:], w["conv_b"].rearrange("(c p) -> p c", p=128), lset, slow=True)
        dma(dtb[:], w["dt_proj_b"].rearrange("(c p) -> p c", p=128), lset, slow=True)
        dma(D_sb[:], w["D_param"].rearrange("(c p) -> p c", p=128), lset, slow=True)
        dma(b1_sb[:].rearrange("p (e c) -> p e c", e=NE), w["b1"].rearrange("e (c p) -> p e c", p=128), lset, slow=True)
        dma(fb0[:], w["ffn_b0"].rearrange("(c p) -> p c", p=128), lset, slow=True)
        dma(fb1[:], w["ffn_b1"].rearrange("(c p) -> p c", p=128), lset, slow=True)
        dma(id_bf[:], w["ident_bf"], lset)
        dma(id_f[:], w["ident_f32"], lset)
        dma(A_sb[:].rearrange("p (c s) -> p c s", s=DS), w["A_log"].rearrange("(c p) s -> p c s", p=128), lset)
        dma(b2_bf[:], w["b2"], gd, q="gpsimd")
        dma(xpw_bf[:].rearrange("p (c m) -> p c m", m=96), w["x_proj_w"].rearrange("(c p) m -> p c m", p=128), gd, q="gpsimd")
        dma(dtw_bf[:], w["dt_proj_w"], gd, q="gpsimd")
        dma(rww_bf[:].rearrange("p (c m) -> p c m", m=NE), w["router_w"].rearrange("(c p) m -> p c m", p=128), gd, q="gpsimd")
        dma(gww_bf[:].rearrange("p (c m) -> p c m", m=NE), w["gate_w"].rearrange("(c p) m -> p c m", p=128), gd, q="gpsimd")
        dma(s_ipbf, w["in_proj_w"], gd, q="gpsimd")
        dma(s_opbf, w["out_proj_w"], gd, q="gpsimd")
        for e_i in range(NE):
            dma(s_w1bf[e_i], w["w1"][e_i], gd, q="gpsimd")
            dma(s_w2bf[e_i], w["w2"][e_i], gd, q="gpsimd")
        P.wait("scalar", lset)
        P.op("scalar", lambda e: e.activation(out=A_sb[:], in_=A_sb[:], func=AF.Exp), se)
        P.wait("vector", se)
        P.op("vector", lambda e: e.tensor_scalar_mul(out=A_sb[:], in0=A_sb[:], scalar1=-1.0), ve)
        P.wait("vector", lset)
        P.op("vector", lambda e: e.memset(ones_col_bf[:], 1.0), ve)
        P.op("vector", lambda e: e.memset(ones_row_f[:], 1.0), ve)

        with nc.psum_tensor("ps_t0", [128, 128], BF16) as ps_t0, \
                nc.sbuf_tensor("wtmp", [128, D], BF16) as wtmp, \
                nc.sbuf_tensor("wttmp", [128, D], BF16) as wttmp:
            for wsrc, sdst in ((w["ffn_w0"], s_w0t), (w["ffn_w1"], s_w1t)):
                for mc in range(KC):
                    P.wait("gpsimd", sta)
                    dma(wtmp[:], wsrc[mc * 128:(mc + 1) * 128, :], lwt, q="gpsimd")
                    for kc2 in range(KC):
                        P.wait("tensor", lwt)
                        P.wait("tensor", ve)
                        P.op("tensor", lambda e, kc2=kc2: e.transpose(
                            out=ps_t0[:], in_=wtmp[:, kc2 * 128:(kc2 + 1) * 128],
                            identity=id_bf[:]), pe)
                        P.wait("vector", pe)
                        P.op("vector", lambda e, kc2=kc2: e.tensor_copy(
                            out=wttmp[:, kc2 * 128:(kc2 + 1) * 128], in_=ps_t0[:]), ve)
                    P.wait("sync", ve)
                    for kc2 in range(KC):
                        store(sdst[kc2, :, mc * 128:(mc + 1) * 128],
                              wttmp[:, kc2 * 128:(kc2 + 1) * 128], sta)
        P.barrier(ALLSEMS)

        def emit_x_load(b, tb, x_fm, xtm, ps_tr):
            t0 = tb * TC
            P.wait("sync", stc)
            cp_ev = {}
            for tt in range(NTT):
                P.wait("sync", pe)
                lv = dma(xtm[tt % 2][:], x_in[b, t0 + tt * 128: t0 + (tt + 1) * 128, :], lx)
                for kc in range(KC):
                    i = tt * KC + kc
                    P.wait("tensor", lx, lv)
                    if i >= 2:
                        P.wait("tensor", ve, cp_ev[i - 2])
                    tr = P.op("tensor", lambda e, tt=tt, kc=kc, i=i: e.transpose(
                        out=ps_tr[i % len(ps_tr)], in_=xtm[tt % 2][:, kc * 128:(kc + 1) * 128],
                        identity=id_f[:]), pe)
                    P.wait("vector", pe, tr)
                    cp_ev[i] = P.op("vector", lambda e, tt=tt, kc=kc, i=i: e.tensor_copy(
                        out=x_fm[:, kc * TC + tt * 128: kc * TC + (tt + 1) * 128],
                        in_=ps_tr[i % len(ps_tr)]), ve)

        def emit_rmsnorm(x_fm, wsb, out_fn, sq_bf, ps_stat, ps_bcst, rrow, eps=1e-6):
            P.wait("scalar", ve)
            P.op("scalar", lambda e: e.activation(out=sq_bf[:], in_=x_fm[:], func=AF.Square), se)
            P.wait("tensor", se)
            P.wait("tensor", ve)
            for kc in range(KC):
                P.op("tensor", lambda e, kc=kc: e.matmul(
                    out=ps_stat[0:1, 0:TC], lhsT=ones_col_bf[:],
                    rhs=sq_bf[:, kc * TC:(kc + 1) * TC],
                    start=(kc == 0), stop=(kc == KC - 1)), pe)
            P.wait("scalar", pe)
            P.op("scalar", lambda e: e.activation(out=rrow[0:1, 0:TC], in_=ps_stat[0:1, 0:TC],
                                                  func=AF.Ln, scale=1.0 / D, bias=eps), se)
            P.op("scalar", lambda e: e.activation(out=rrow[0:1, 0:TC], in_=rrow[0:1, 0:TC],
                                                  func=AF.Exp, scale=-0.5), se)
            P.wait("tensor", se)
            P.op("tensor", lambda e: e.matmul(out=ps_bcst[:, 0:TC], lhsT=ones_row_f[:],
                                              rhs=rrow[0:1, 0:TC], start=True, stop=True), pe)
            P.wait("vector", pe)
            for kc in range(KC):
                P.op("vector", lambda e, kc=kc: e.scalar_tensor_tensor(
                    out=out_fn(kc), in0=x_fm[:, kc * TC:(kc + 1) * TC],
                    scalar=wsb[:, kc:kc + 1], in1=ps_bcst[:, 0:TC],
                    op0=ALU.mult, op1=ALU.mult), ve)

        def emit_layernorm(xin, out_f, cast_bf, sq_bf, ps_stat, ps_bc, rows, trow, eps=1e-5):
            P.wait("vector", se)
            P.op("vector", lambda e: e.tensor_copy(out=cast_bf[:], in_=xin[:]), ve)
            P.wait("scalar", ve)
            P.op("scalar", lambda e: e.activation(out=sq_bf[:], in_=xin[:], func=AF.Square), se)
            P.wait("tensor", se)
            P.wait("tensor", ve)
            for kc in range(KC):
                P.op("tensor", lambda e, kc=kc: e.matmul(
                    out=ps_stat[0:1, 0:TC], lhsT=ones_col_bf[:],
                    rhs=cast_bf[:, kc * TC:(kc + 1) * TC],
                    start=(kc == 0), stop=(kc == KC - 1)), pe)
            for kc in range(KC):
                P.op("tensor", lambda e, kc=kc: e.matmul(
                    out=ps_stat[0:1, TC:2 * TC], lhsT=ones_col_bf[:],
                    rhs=sq_bf[:, kc * TC:(kc + 1) * TC],
                    start=(kc == 0), stop=(kc == KC - 1)), pe)
            P.wait("scalar", pe)
            P.op("scalar", lambda e: e.activation(out=rows[0:1, 0:TC], in_=ps_stat[0:1, 0:TC],
                                                  func=AF.Copy, scale=1.0 / D), se)
            P.op("scalar", lambda e: e.activation(out=rows[0:1, TC:2 * TC],
                                                  in_=ps_stat[0:1, TC:2 * TC],
                                                  func=AF.Copy, scale=1.0 / D), se)
            P.wait("vector", se)
            P.op("vector", lambda e: e.tensor_mul(out=trow[0:1, 0:TC], in0=rows[0:1, 0:TC],
                                                  in1=rows[0:1, 0:TC]), ve)
            P.op("vector", lambda e: e.tensor_sub(out=trow[0:1, 0:TC],
                                                  in0=rows[0:1, TC:2 * TC],
                                                  in1=trow[0:1, 0:TC]), ve)
            P.wait("scalar", ve)
            P.op("scalar", lambda e: e.activation(out=trow[0:1, 0:TC],
                                                  in_=trow[0:1, 0:TC],
                                                  func=AF.Ln, bias=eps), se)
            P.op("scalar", lambda e: e.activation(out=trow[0:1, 0:TC],
                                                  in_=trow[0:1, 0:TC],
                                                  func=AF.Exp, scale=-0.5), se)
            P.wait("vector", se)
            P.op("vector", lambda e: e.tensor_mul(out=rows[0:1, TC:2 * TC],
                                                  in0=rows[0:1, 0:TC],
                                                  in1=trow[0:1, 0:TC]), ve)
            P.op("vector", lambda e: e.tensor_copy(out=rows[0:1, 0:TC],
                                                   in_=trow[0:1, 0:TC]), ve)
            P.wait("tensor", ve)
            P.op("tensor", lambda e: e.matmul(out=ps_bc[0][:, 0:TC], lhsT=ones_row_f[:],
                                              rhs=rows[0:1, 0:TC], start=True, stop=True), pe)
            P.op("tensor", lambda e: e.matmul(out=ps_bc[1][:, 0:TC], lhsT=ones_row_f[:],
                                              rhs=rows[0:1, TC:2 * TC], start=True, stop=True), pe)
            P.wait("vector", pe)
            for kc in range(KC):
                P.op("vector", lambda e, kc=kc: e.tensor_mul(
                    out=out_f[:, kc * TC:(kc + 1) * TC], in0=xin[:, kc * TC:(kc + 1) * TC],
                    in1=ps_bc[0][:, 0:TC]), ve)
            for kc in range(KC):
                P.op("vector", lambda e, kc=kc: e.tensor_sub(
                    out=out_f[:, kc * TC:(kc + 1) * TC], in0=out_f[:, kc * TC:(kc + 1) * TC],
                    in1=ps_bc[1][:, 0:TC]), ve)

        def emit_topk_gates(lgtm, gtm, tk, se_ref):
            col = lambda tt, i: tk[:, tt * 20 + i: tt * 20 + i + 1]
            lgm_ = lambda tt: tk[:, tt * 20 + 9: tt * 20 + 17]
            lg8_ = lambda tt: lgtm[:, tt * 8:(tt + 1) * 8]
            g8_ = lambda tt: gtm[:, tt * 8:(tt + 1) * 8]
            for tt in range(NTT):
                P.op("vector", lambda e, tt=tt: e.reduce_max(
                    out=col(tt, 0), in_=lg8_(tt), axis=AX), ve)
            for tt in range(NTT):
                P.op("vector", lambda e, tt=tt: e.tensor_scalar(
                    out=g8_(tt), in0=lg8_(tt), scalar1=col(tt, 0), scalar2=None,
                    op0=ALU.is_equal), ve)
            for tt in range(NTT):
                P.op("vector", lambda e, tt=tt: e.scalar_tensor_tensor(
                    out=lgm_(tt), in0=g8_(tt), scalar=-1e30, in1=lg8_(tt),
                    op0=ALU.mult, op1=ALU.add), ve)
            for tt in range(NTT):
                P.op("vector", lambda e, tt=tt: e.reduce_max(
                    out=col(tt, 1), in_=lgm_(tt), axis=AX), ve)
            for tt in range(NTT):
                dv = P.op("vector", lambda e, tt=tt: e.tensor_sub(
                    out=col(tt, 2), in0=col(tt, 1), in1=col(tt, 0)), ve)
            P.wait("scalar", ve, dv)
            for tt in range(NTT):
                sv = P.op("scalar", lambda e, tt=tt: e.activation(
                    out=col(tt, 3), in_=col(tt, 2), func=AF.Sigmoid, scale=-1.0), se)
            P.wait("vector", se, sv)
            for tt in range(NTT):
                P.op("vector", lambda e, tt=tt: e.tensor_scalar(
                    out=col(tt, 4), in0=col(tt, 3), scalar1=-1.0, scalar2=1.0,
                    op0=ALU.mult, op1=ALU.add), ve)
            for tt in range(NTT):
                P.op("vector", lambda e, tt=tt: e.tensor_scalar_mul(
                    out=g8_(tt), in0=g8_(tt), scalar1=col(tt, 3)), ve)
            for tt in range(NTT):
                P.op("vector", lambda e, tt=tt: e.tensor_scalar(
                    out=lgm_(tt), in0=lgm_(tt), scalar1=col(tt, 1), scalar2=None,
                    op0=ALU.is_equal), ve)
            for tt in range(NTT):
                P.op("vector", lambda e, tt=tt: e.scalar_tensor_tensor(
                    out=g8_(tt), in0=lgm_(tt), scalar=col(tt, 4), in1=g8_(tt),
                    op0=ALU.mult, op1=ALU.add), ve)

        def emit_logit_transpose(lg_sb, lgtm, ps_tr):
            P.wait("tensor", ve)
            for tt in range(NTT):
                P.op("tensor", lambda e, tt=tt: e.transpose(
                    out=ps_tr[0][0:128, 0:8], in_=lg_sb[:, tt * 128:(tt + 1) * 128],
                    identity=id_f[0:8, 0:8]), pe)
                P.wait("vector", pe)
                P.op("vector", lambda e, tt=tt: e.tensor_copy(
                    out=lgtm[:, tt * 8:(tt + 1) * 8], in_=ps_tr[0][0:128, 0:8]), ve)
                P.wait("tensor", ve)

        def emit_rows_back(gtm, gfm, nrows, ps_tr):
            P.wait("tensor", ve)
            for tt in range(NTT):
                P.op("tensor", lambda e, tt=tt: e.transpose(
                    out=ps_tr[0][0:8, 0:128], in_=gtm[:, tt * 8:(tt + 1) * 8],
                    identity=id_f[:]), pe)
                P.wait("vector", pe)
                P.op("vector", lambda e, tt=tt, nrows=nrows: e.tensor_copy(
                    out=gfm[0:nrows, tt * 128:(tt + 1) * 128],
                    in_=ps_tr[0][0:nrows, 0:128]), ve)
                P.wait("tensor", ve)

        def phase_A(b):
            with ExitStack() as actx:
                AE = actx.enter_context
                inproj_sb = AE(nc.sbuf_tensor(f"inproj_sb_{b}", [128, KC * 4096], BF16))
                diag_all = AE(nc.sbuf_tensor(f"diag_all_{b}", [128, NDC * 4 * 128], BF16))
                x_fm = AE(nc.sbuf_tensor(f"xa_fm_{b}", [128, KC * TC], F32))
                xtm = [AE(nc.sbuf_tensor(f"xtm{i}_{b}", [128, D], F32)) for i in range(2)]
                sq_bf = AE(nc.sbuf_tensor(f"sqa_bf_{b}", [128, KC * TC], BF16))
                h_bf = AE(nc.sbuf_tensor(f"h_bf_{b}", [128, KC * TC], BF16))
                rrow = AE(nc.sbuf_tensor(f"rrow_a_{b}", [1, 2 * TC], F32))
                xcraw = AE(nc.sbuf_tensor(f"xcraw_{b}", [128, NDC * TC], BF16))
                xcs_bf = AE(nc.sbuf_tensor(f"xcs_bf_{b}", [128, NDC * TC], BF16))
                ztile = [AE(nc.sbuf_tensor(f"ztile{i}_{b}", [128, TC], F32)) for i in range(2)]
                xtile = [AE(nc.sbuf_tensor(f"xtile{i}_{b}", [128, TC], F32)) for i in range(2)]
                dtile = [AE(nc.sbuf_tensor(f"dtile{i}_{b}", [128, TC], F32)) for i in range(2)]
                proj_sb = AE(nc.sbuf_tensor(f"proj_sb_{b}", [96, TC], F32))
                dtr_bf = AE(nc.sbuf_tensor(f"dtr_bf_{b}", [64, TC], BF16))
                sp1 = [AE(nc.sbuf_tensor(f"sp1{i}_{b}", [128, TC], F32)) for i in range(2)]
                sp2 = [AE(nc.sbuf_tensor(f"sp2{i}_{b}", [128, TC], F32)) for i in range(2)]
                spr = [AE(nc.sbuf_tensor(f"spr{i}_{b}", [128, TC], F32)) for i in range(2)]
                ps_mm = [AE(nc.psum_tensor(f"ps_mma{i}_{b}", [128, TC], F32)) for i in range(2)]
                ps_cv = [AE(nc.psum_tensor(f"ps_cv{i}_{b}", [128, TC], F32)) for i in range(2)]
                ps_xp = AE(nc.psum_tensor(f"ps_xp_{b}", [96, TC], F32))
                ps_st = AE(nc.psum_tensor(f"ps_sta_{b}", [128, TC], F32))
                ps_tr_t = [AE(nc.psum_tensor(f"ps_tra{i}_{b}", [128, 128], F32)) for i in range(2)]
                ps_tr = [t[:] for t in ps_tr_t]

                for kc in range(KC):
                    dma(inproj_sb[:, kc * 4096:(kc + 1) * 4096],
                        s_ipbf[kc * 128:(kc + 1) * 128, :], lw)
                P.wait("vector", lset)
                for dc in range(NDC):
                    for k in range(4):
                        P.op("vector", lambda e, dc=dc, k=k: e.tensor_scalar_mul(
                            out=diag_all[:, (dc * 4 + k) * 128:(dc * 4 + k + 1) * 128],
                            in0=id_bf[:], scalar1=convw[:, dc * 4 + k: dc * 4 + k + 1]), ve)
                hist0_ev = P.op("vector", lambda e: e.memset(hist_bf[:], 0.0), ve)
                hist_ev = {dc: hist0_ev for dc in range(NDC)}

                cv_evac_ev = {}
                for tb in range(NTB):
                    t0 = tb * TC
                    emit_x_load(b, tb, x_fm, xtm, ps_tr)
                    emit_rmsnorm(x_fm, n1w,
                                 lambda kc: h_bf[:, kc * TC:(kc + 1) * TC],
                                 sq_bf, ps_st, ps_mm[0], rrow)
                    norm_ev = ve.n
                    if tb == 0:
                        P.wait("tensor", lw)
                    ip_ev = {}
                    for m in range(32):
                        pb = ps_mm[m % 2]
                        if m < 2:
                            P.wait("tensor", ve, norm_ev)
                        else:
                            P.wait("tensor", se, ip_ev[m - 2])
                        mm_ev = None
                        for kc in range(KC):
                            mm_ev = P.op("tensor", lambda e, m=m, kc=kc, pb=pb: e.matmul(
                                out=pb[:, 0:TC],
                                lhsT=inproj_sb[:, kc * 4096 + m * 128: kc * 4096 + (m + 1) * 128],
                                rhs=h_bf[:, kc * TC:(kc + 1) * TC],
                                start=(kc == 0), stop=(kc == KC - 1)), pe)
                        P.wait("scalar", pe, mm_ev)
                        if m < 16:
                            ip_ev[m] = P.op("scalar", lambda e, m=m, pb=pb: e.activation(
                                out=xcraw[:, m * TC:(m + 1) * TC], in_=pb[:, 0:TC],
                                func=AF.Copy), se)
                        else:
                            zs = m % 2
                            zt = ztile[zs]
                            P.wait("scalar", stz2[zs])
                            ip_ev[m] = P.op("scalar", lambda e, pb=pb, zt=zt: e.activation(
                                out=zt[:], in_=pb[:, 0:TC], func=AF.Silu), se)
                            P.wait("sync", se, ip_ev[m])
                            store(s_z[b, m - 16, :, t0:t0 + TC], zt[:], stz2[zs])
                    xp_ev = None
                    for dc in range(NDC):
                        pcv = ps_cv[dc % 2]
                        cur = xcraw[:, dc * TC:(dc + 1) * TC]
                        hist = hist_bf[:, dc * 3:(dc + 1) * 3]
                        dgs = diag_all[:, dc * 4 * 128:(dc + 1) * 4 * 128]
                        P.wait("tensor", se, ip_ev[dc])
                        if dc >= 2:
                            P.wait("tensor", se, cv_evac_ev[dc - 2])
                        P.wait("tensor", ve, hist_ev[dc])
                        P.op("tensor", lambda e, cur=cur, dgs=dgs, pcv=pcv: e.matmul(
                            out=pcv[:, 0:TC], lhsT=dgs[:, 3 * 128:4 * 128], rhs=cur,
                            start=True, stop=False), pe)
                        cv_mm = None
                        for k in range(3):
                            nh = 3 - k
                            P.op("tensor", lambda e, k=k, nh=nh, hist=hist, dgs=dgs, pcv=pcv: e.matmul(
                                out=pcv[:, 0:nh], lhsT=dgs[:, k * 128:(k + 1) * 128],
                                rhs=hist[:, k:3], start=False, stop=False), pe)
                            cv_mm = P.op("tensor", lambda e, k=k, nh=nh, cur=cur, dgs=dgs, pcv=pcv: e.matmul(
                                out=pcv[:, nh:TC], lhsT=dgs[:, k * 128:(k + 1) * 128],
                                rhs=cur[:, 0:TC - nh], start=False, stop=(k == 2)), pe)
                        P.wait("vector", pe, cv_mm)
                        hist_ev[dc] = P.op("vector", lambda e, cur=cur, hist=hist: e.tensor_copy(
                            out=hist, in_=cur[:, TC - 3:TC]), ve)
                        xs = dc % 2
                        xt = xtile[xs]
                        P.wait("scalar", pe, cv_mm)
                        P.wait("scalar", stx2[xs])
                        cv_evac_ev[dc] = P.op("scalar", lambda e, dc=dc, xt=xt, pcv=pcv: e.activation(
                            out=xt[:], in_=pcv[:, 0:TC], func=AF.Silu,
                            bias=convb[:, dc:dc + 1]), se)
                        P.wait("sync", se, cv_evac_ev[dc])
                        store(s_xc[b, dc, :, t0:t0 + TC], xt[:], stx2[xs])
                        P.wait("vector", se, cv_evac_ev[dc])
                        xcs_ev = P.op("vector", lambda e, dc=dc, xt=xt: e.tensor_copy(
                            out=xcs_bf[:, dc * TC:(dc + 1) * TC], in_=xt[:]), ve)
                        P.wait("tensor", ve, xcs_ev)
                        xp_ev = P.op("tensor", lambda e, dc=dc: e.matmul(
                            out=ps_xp[:, 0:TC], lhsT=xpw_bf[:, dc * 96:(dc + 1) * 96],
                            rhs=xcs_bf[:, dc * TC:(dc + 1) * TC],
                            start=(dc == 0), stop=(dc == NDC - 1)), pe)
                    P.wait("vector", pe, xp_ev)
                    P.op("vector", lambda e: e.tensor_copy(out=proj_sb[:], in_=ps_xp[0:96, 0:TC]), ve)
                    dtr_ev = P.op("vector", lambda e: e.tensor_copy(out=dtr_bf[:], in_=proj_sb[0:64, :]), ve)
                    P.wait("sync", ve, dtr_ev)
                    P.wait("sync", sta)
                    store(s_bc[b, :, t0:t0 + TC], proj_sb[64:96, :], sta)
                    abs_ev, ln_ev, relu_ev, add_ev = {}, {}, {}, {}
                    for m in range(NDC):
                        ms = m % 2
                        pb = ps_mm[ms]
                        if m < 2:
                            P.wait("tensor", ve, dtr_ev)
                            P.wait("tensor", se, ip_ev[30 + m])
                        else:
                            P.wait("tensor", se, abs_ev[m - 2])
                            P.wait("tensor", ve, relu_ev[m - 2])
                        dmm = P.op("tensor", lambda e, m=m, pb=pb: e.matmul(
                            out=pb[:, 0:TC], lhsT=dtw_bf[:, m * 128:(m + 1) * 128],
                            rhs=dtr_bf[:], start=True, stop=True), pe)
                        P.wait("scalar", pe, dmm)
                        if m >= 2:
                            P.wait("scalar", ve, add_ev[m - 2])
                        abs_ev[m] = P.op("scalar", lambda e, m=m, pb=pb, ms=ms: e.activation(
                            out=sp1[ms][:], in_=pb[:, 0:TC], func=AF.Abs,
                            bias=dtb[:, m:m + 1]), se)
                        P.op("scalar", lambda e, ms=ms: e.activation(
                            out=sp2[ms][:], in_=sp1[ms][:], func=AF.Exp, scale=-1.0), se)
                        ln_ev[m] = P.op("scalar", lambda e, ms=ms: e.activation(
                            out=sp1[ms][:], in_=sp2[ms][:], func=AF.Ln, bias=1.0), se)
                        P.wait("vector", se, abs_ev[m])
                        relu_ev[m] = P.op("vector", lambda e, m=m, pb=pb, ms=ms: e.tensor_scalar(
                            out=spr[ms][:], in0=pb[:, 0:TC], scalar1=dtb[:, m:m + 1],
                            scalar2=0.0, op0=ALU.add, op1=ALU.max), ve)
                        P.wait("vector", se, ln_ev[m])
                        P.wait("vector", std2[ms])
                        dt_t = dtile[ms]
                        add_ev[m] = P.op("vector", lambda e, dt_t=dt_t, ms=ms: e.tensor_add(
                            out=dt_t[:], in0=spr[ms][:], in1=sp1[ms][:]), ve)
                        P.wait("sync", ve, add_ev[m])
                        store(s_dt[b, m, :, t0:t0 + TC], dt_t[:], std2[ms])
                P.barrier(ALLSEMS)

        def phase_B(b):
            with ExitStack() as bctx:
                BE = bctx.enter_context
                cache = BE(nc.sbuf_tensor(f"bc_cache_{b}", [128, 2 * DS * TC], F32))
                dtt = [BE(nc.sbuf_tensor(f"dtt{i}_{b}", [128, TC], F32)) for i in range(2)]
                xct = [BE(nc.sbuf_tensor(f"xct{i}_{b}", [128, TC], F32)) for i in range(2)]
                ztt = [BE(nc.sbuf_tensor(f"ztt{i}_{b}", [128, TC], F32)) for i in range(2)]
                u_t = BE(nc.sbuf_tensor(f"u_t_{b}", [128, TC], F32))
                dA = [BE(nc.sbuf_tensor(f"dA{i}_{b}", [128, TC], F32)) for i in range(3)]
                b_all = BE(nc.sbuf_tensor(f"b_all_{b}", [128, DS * TC], F32))
                h_all = BE(nc.sbuf_tensor(f"h_all_{b}", [128, DS * TC], F32))
                mc_all = BE(nc.sbuf_tensor(f"mc_all_{b}", [128, DS * TC], BF16))
                yt = [BE(nc.sbuf_tensor(f"yt{i}_{b}", [128, TC], F32)) for i in range(2)]
                ymix = BE(nc.sbuf_tensor(f"ymix_{b}", [128, NDC * TC], BF16))
                opw = [BE(nc.sbuf_tensor(f"opw{i}_{b}", [128, NDC * 128], BF16)) for i in range(2)]
                ps_y = [BE(nc.psum_tensor(f"ps_y{i}_{b}", [128, TC], F32)) for i in range(2)]
                ps_op = [BE(nc.psum_tensor(f"ps_op{i}_{b}", [128, TC], F32)) for i in range(2)]

                for tb in range(NTB):
                    t0 = tb * TC
                    P.wait("sync", ve)
                    for row in range(2 * DS):
                        dma(cache[:, row * TC:(row + 1) * TC],
                            s_bc[b, row:row + 1, t0:t0 + TC].partition_broadcast(128), lbc)
                    P.wait("vector", lbc)
                    tts_ve = {}
                    id_ev, yt_ev, ymix_ev, exps_ev, lact_ev, mc_ev = {}, {}, {}, {}, {}, {}

                    def emit_loads(dc):
                        if dc >= NDC:
                            return
                        io = dc % 2
                        if dc >= 2:
                            P.wait("sync", ve, ymix_ev[dc - 2])
                            P.wait("sync", se, exps_ev[dc - 2])
                        dma(dtt[io][:], s_dt[b, dc, :, t0:t0 + TC], lact2[io])
                        dma(xct[io][:], s_xc[b, dc, :, t0:t0 + TC], lact2[io])
                        dma(ztt[io][:], s_z[b, dc, :, t0:t0 + TC], lact2[io])
                        lact_ev[dc] = lact2[io].n

                    def emit_ytail(dc):
                        io = dc % 2
                        P.wait("vector", pe, id_ev[dc])
                        yt_ev[dc] = P.op("vector", lambda e, io=io, dc=dc: e.scalar_tensor_tensor(
                            out=yt[io][:], in0=xct[io][:], scalar=D_sb[:, dc:dc + 1],
                            in1=ps_y[dc % 2][:, 0:TC], op0=ALU.mult, op1=ALU.add), ve)
                        ymix_ev[dc] = P.op("vector", lambda e, io=io, dc=dc: e.tensor_mul(
                            out=ymix[:, dc * TC:(dc + 1) * TC], in0=yt[io][:],
                            in1=ztt[io][:]), ve)

                    emit_loads(0)
                    emit_loads(1)
                    for dc in range(NDC):
                        io = dc % 2
                        P.wait("vector", lact2[io], lact_ev[dc])
                        P.op("vector", lambda e, io=io: e.tensor_mul(
                            out=u_t[:], in0=dtt[io][:], in1=xct[io][:]), ve)
                        b_ev = P.op("vector", lambda e: e.tensor_tensor(
                            out=b_all[:].rearrange("p (s t) -> p s t", s=DS),
                            in0=u_t[:].unsqueeze(1).broadcast_to([128, DS, TC]),
                            in1=cache[:, 0:DS * TC].rearrange("p (s t) -> p s t", s=DS),
                            op=ALU.mult), ve)
                        if dc >= 1:
                            emit_ytail(dc - 1)
                            emit_loads(dc + 1)
                        P.wait("scalar", lact2[io], lact_ev[dc])
                        for si in range(DS):
                            da = dA[si % 3]
                            if si >= 3:
                                P.wait("scalar", ve, tts_ve[si - 3])
                            elif dc >= 1:
                                P.wait("scalar", ve, tts_ve[DS - 1])
                            exp_val = P.op("scalar", lambda e, io=io, dc=dc, si=si, da=da: e.activation(
                                out=da[:], in_=dtt[io][:], func=AF.Exp,
                                scale=A_sb[:, dc * DS + si: dc * DS + si + 1]), se)
                            P.wait("vector", se, exp_val)
                            if tb == 0:
                                tts_ve[si] = P.op("vector", lambda e, si=si, da=da: e.tensor_tensor_scan(
                                    out=h_all[:, si * TC:(si + 1) * TC], data0=da[:],
                                    data1=b_all[:, si * TC:(si + 1) * TC],
                                    initial=0.0, op0=ALU.mult, op1=ALU.add), ve)
                            else:
                                tts_ve[si] = P.op("vector", lambda e, dc=dc, si=si, da=da: e.tensor_tensor_scan(
                                    out=h_all[:, si * TC:(si + 1) * TC], data0=da[:],
                                    data1=b_all[:, si * TC:(si + 1) * TC],
                                    initial=state_sb[:, dc * DS + si: dc * DS + si + 1],
                                    op0=ALU.mult, op1=ALU.add), ve)
                        exps_ev[dc] = se.n
                        if tb < NTB - 1:
                            P.op("vector", lambda e, dc=dc: e.tensor_copy(
                                out=state_sb[:, dc * DS:(dc + 1) * DS].unsqueeze(2),
                                in_=h_all[:].rearrange("p (s t) -> p s t", s=DS)[:, :, TC - 1:TC]), ve)
                        mc_ev[dc] = P.op("vector", lambda e: e.tensor_tensor(
                            out=mc_all[:].rearrange("p (s t) -> p s t", s=DS),
                            in0=h_all[:].rearrange("p (s t) -> p s t", s=DS),
                            in1=cache[:, DS * TC:2 * DS * TC].rearrange("p (s t) -> p s t", s=DS),
                            op=ALU.mult), ve)
                        py = ps_y[dc % 2]
                        P.wait("tensor", ve, mc_ev[dc])
                        if dc >= 2:
                            P.wait("tensor", ve, yt_ev[dc - 2])
                        for si in range(DS):
                            id_ev[dc] = P.op("tensor", lambda e, si=si, py=py: e.matmul(
                                out=py[:, 0:TC], lhsT=id_bf[:],
                                rhs=mc_all[:, si * TC:(si + 1) * TC],
                                start=(si == 0), stop=(si == DS - 1)), pe)
                    emit_ytail(NDC - 1)
                    opmm_ev, mixcp_ev = {}, {}
                    lw2_ev = {}
                    lw2_ev[0] = dma(opw[0][:].rearrange("p (c n) -> p c n", n=128),
                                    s_opbf[:, 0:128].rearrange("(c p) n -> p c n", p=128), lw2[0], q="gpsimd")
                    for m in range(KC):
                        ms = m % 2
                        po = ps_op[ms]
                        if m + 1 < KC:
                            if m >= 1:
                                P.wait("gpsimd", pe, opmm_ev[m - 1])
                            lw2_ev[(m + 1) % 2] = dma(
                                opw[(m + 1) % 2][:].rearrange("p (c n) -> p c n", n=128),
                                s_opbf[:, (m + 1) * 128:(m + 2) * 128].rearrange(
                                    "(c p) n -> p c n", p=128), lw2[(m + 1) % 2], q="gpsimd")
                        P.wait("tensor", lw2[ms], lw2_ev[ms])
                        if m < 2:
                            P.wait("tensor", ve, ymix_ev[NDC - 1])
                        else:
                            P.wait("tensor", ve, mixcp_ev[m - 2])
                        for dc in range(NDC):
                            opmm_ev[m] = P.op("tensor", lambda e, m=m, dc=dc, po=po, ms=ms: e.matmul(
                                out=po[:, 0:TC], lhsT=opw[ms][:, dc * 128:(dc + 1) * 128],
                                rhs=ymix[:, dc * TC:(dc + 1) * TC],
                                start=(dc == 0), stop=(dc == NDC - 1)), pe)
                        P.wait("vector", pe, opmm_ev[m])
                        P.wait("vector", stb2[ms])
                        mt = yt[ms]
                        mixcp_ev[m] = P.op("vector", lambda e, po=po, mt=mt: e.tensor_copy(
                            out=mt[:], in_=po[:, 0:TC]), ve)
                        P.wait("sync", ve, mixcp_ev[m])
                        store(s_mix[b, m, :, t0:t0 + TC], mt[:], stb2[ms])
                P.barrier(ALLSEMS)

        def phase_C(b):
            with ExitStack() as cctx:
                CE = cctx.enter_context
                x_fm = CE(nc.sbuf_tensor(f"xc_fm_{b}", [128, KC * TC], F32))
                xtm = [CE(nc.sbuf_tensor(f"xtmc{i}_{b}", [128, D], F32)) for i in range(2)]
                mix_fm = CE(nc.sbuf_tensor(f"mix_fm_{b}", [128, KC * TC], F32))
                x2_bf = CE(nc.sbuf_tensor(f"x2_bf_{b}", [128, KC * TC], BF16))
                sq_bf = CE(nc.sbuf_tensor(f"sqc_bf_{b}", [128, KC * TC], BF16))
                rrow = CE(nc.sbuf_tensor(f"rrow_c_{b}", [1, 2 * TC], F32))
                trow = CE(nc.sbuf_tensor(f"trow_c_{b}", [1, TC], F32))
                ht_all = CE(nc.sbuf_tensor(f"ht_all_{b}", [128, KC * 2 * TC], BF16))
                lg_sb = CE(nc.sbuf_tensor(f"lg_sb_{b}", [8, TC], F32))
                lgtm = CE(nc.sbuf_tensor(f"lgtm_{b}", [128, NTT * 8], F32))
                tk = CE(nc.sbuf_tensor(f"tk_{b}", [128, NTT * 20], F32))
                gtm = CE(nc.sbuf_tensor(f"gtm_{b}", [128, NTT * 8], F32))
                gfm = CE(nc.sbuf_tensor(f"gfm_{b}", [8, TC], F32))
                gfm_bf = CE(nc.sbuf_tensor(f"gfm_bf_{b}", [8, TC], BF16))
                wm = [CE(nc.sbuf_tensor(f"wmc{i}_{b}", [128, EC * 128], BF16)) for i in range(2)]
                b1row = CE(nc.sbuf_tensor(f"b1row_{b}", [1, EDIM], BF16))
                ones_tc = CE(nc.sbuf_tensor(f"ones_tc_{b}", [1, TC], BF16))
                ghid = CE(nc.sbuf_tensor(f"ghid_{b}", [128, 2 * EC * TC], BF16))
                sqg = [CE(nc.sbuf_tensor(f"sqg{i}_{b}", [128, TC], F32)) for i in range(2)]
                ug = [CE(nc.sbuf_tensor(f"ug{i}_{b}", [128, TC], F32)) for i in range(2)]
                gbc = [CE(nc.sbuf_tensor(f"gbc{i}_{b}", [128, TC], F32)) for i in range(2)]
                moe_sb = CE(nc.sbuf_tensor(f"moe_sb_{b}", [128, 2 * KC * TC], F32))
                ft = [CE(nc.sbuf_tensor(f"ftc{i}_{b}", [128, TC], F32)) for i in range(2)]
                f_sb = CE(nc.sbuf_tensor(f"f_sb_{b}", [128, KC * TC], F32))
                ps_mm = [CE(nc.psum_tensor(f"ps_cm{i}_{b}", [128, TC], F32)) for i in range(4)]
                ps_st = CE(nc.psum_tensor(f"ps_cst_{b}", [128, 2 * TC], F32))
                ps_lg = CE(nc.psum_tensor(f"ps_lg_{b}", [8, TC], F32))
                ps_tr = [ps_mm[2][:, 0:128], ps_mm[3][:, 0:128]]
                ps_b2 = ps_mm[2]

                P.op("vector", lambda e: e.memset(ones_tc[:], 1.0), ve)
                for th in range(2):
                    tbs = [2 * th, 2 * th + 1]
                    for j, tb in enumerate(tbs):
                        t0 = tb * TC
                        emit_x_load(b, tb, x_fm, xtm, ps_tr)
                        P.wait("sync", ve)
                        for kc in range(KC):
                            dma(mix_fm[:, kc * TC:(kc + 1) * TC], s_mix[b, kc, :, t0:t0 + TC], lmix)
                        P.wait("vector", lmix)
                        P.wait("vector", stb)
                        for kc in range(KC):
                            P.op("vector", lambda e, kc=kc: e.tensor_add(
                                out=x_fm[:, kc * TC:(kc + 1) * TC],
                                in0=x_fm[:, kc * TC:(kc + 1) * TC],
                                in1=mix_fm[:, kc * TC:(kc + 1) * TC]), ve)
                        emit_rmsnorm(x_fm, n2w,
                                     lambda kc, j=j: ht_all[:, (kc * 2 + j) * TC:(kc * 2 + j + 1) * TC],
                                     sq_bf, ps_st, ps_mm[0], rrow)
                        P.wait("tensor", ve)
                        for kc in range(KC):
                            P.op("tensor", lambda e, kc=kc, j=j: e.matmul(
                                out=ps_lg[:, 0:TC], lhsT=rww_bf[:, kc * 8:(kc + 1) * 8],
                                rhs=ht_all[:, (kc * 2 + j) * TC:(kc * 2 + j + 1) * TC],
                                start=(kc == 0), stop=(kc == KC - 1)), pe)
                        P.wait("vector", pe)
                        P.op("vector", lambda e: e.tensor_copy(out=lg_sb[:], in_=ps_lg[0:8, 0:TC]), ve)
                        emit_logit_transpose(lg_sb, lgtm, ps_tr)
                        emit_topk_gates(lgtm, gtm, tk, se)
                        P.wait("vector", stc)
                        emit_rows_back(gtm, gfm, 8, ps_tr)
                        P.wait("sync", ve)
                        store(s_gates[b, tb], gfm[:], stc)
                    P.op("vector", lambda e: e.memset(moe_sb[:], 0.0), ve)
                    w1v = s_w1bf.rearrange("e (c p) m -> e p c m", p=128)
                    w2v = s_w2bf.rearrange("e (c p) m -> e p c m", p=128)
                    wm_user = {0: 0, 1: 0}
                    b1_user = 0
                    for e_i in range(NE):
                        P.wait("gpsimd", pe, b1_user)
                        dma(b1row[:], w["b1"][e_i:e_i + 1, :], gd, q="gpsimd")
                        b1_gd = gd.n
                        gh_ev = {}
                        ug_ev = {}
                        for hc in range(EC):
                            slot = hc % 2
                            P.wait("gpsimd", pe, wm_user[slot])
                            wld = dma(wm[slot][:, 0:KC * 128].rearrange("p (c n) -> p c n", n=128),
                                      w1v[e_i][:, :, hc * 128:(hc + 1) * 128], lw2[slot], q="gpsimd")
                            for j, tb in enumerate(tbs):
                                pb = ps_mm[(hc * 2 + j) % 4]
                                P.wait("tensor", lw2[slot], wld)
                                if hc == 0 and j == 0:
                                    P.wait("tensor", gd, b1_gd)
                                if hc >= 2:
                                    P.wait("tensor", ve, gh_ev[(hc - 2, j)])
                                for kc in range(KC):
                                    P.op("tensor", lambda e, hc=hc, kc=kc, j=j, slot=slot, pb=pb: e.matmul(
                                        out=pb[:, 0:TC],
                                        lhsT=wm[slot][:, kc * 128:(kc + 1) * 128],
                                        rhs=ht_all[:, (kc * 2 + j) * TC:(kc * 2 + j + 1) * TC],
                                        start=(kc == 0), stop=False), pe)
                                mm_ev = P.op("tensor", lambda e, hc=hc, pb=pb: e.matmul(
                                    out=pb[:, 0:TC], lhsT=b1row[0:1, hc * 128:(hc + 1) * 128],
                                    rhs=ones_tc[0:1, 0:TC], start=False, stop=True), pe)
                                wm_user[slot] = mm_ev
                                b1_user = mm_ev
                                P.wait("scalar", pe, mm_ev)
                                if hc >= 1:
                                    P.wait("scalar", ve, ug_ev[(hc - 1, j)])
                                sqv = P.op("scalar", lambda e, pb=pb, j=j: e.activation(
                                    out=sqg[j][:], in_=pb[:, 0:TC], func=AF.Square,
                                    scale=SQRT_GK2), se)
                                P.wait("vector", se, sqv)
                                ug_ev[(hc, j)] = P.op("vector", lambda e, pb=pb, j=j: e.scalar_tensor_tensor(
                                    out=ug[j][:], in0=sqg[j][:], scalar=GK1, in1=pb[:, 0:TC],
                                    op0=ALU.add, op1=ALU.mult), ve)
                                P.wait("scalar", ve, ug_ev[(hc, j)])
                                sgv = P.op("scalar", lambda e, j=j: e.activation(
                                    out=ug[j][:], in_=ug[j][:], func=AF.Sigmoid), se)
                                P.wait("vector", se, sgv)
                                gh_ev[(hc, j)] = P.op("vector", lambda e, j=j, hc=hc, pb=pb: e.tensor_mul(
                                    out=ghid[:, (j * EC + hc) * TC:(j * EC + hc + 1) * TC],
                                    in0=ug[j][:], in1=pb[:, 0:TC]), ve)
                        P.wait("sync", ve)
                        P.wait("sync", stc)
                        for j, tb in enumerate(tbs):
                            dma(gbc[j][:], s_gates[b, tb, e_i:e_i + 1, :].partition_broadcast(128), lbc)
                        gbc_ev = lbc.n
                        madd_ev = {}
                        for m in range(KC):
                            slot = m % 2
                            P.wait("gpsimd", pe, wm_user[slot])
                            wld = dma(wm[slot][:, 0:EC * 128].rearrange("p (c n) -> p c n", n=128),
                                      w2v[e_i][:, :, m * 128:(m + 1) * 128], lw2[slot], q="gpsimd")
                            for j, tb in enumerate(tbs):
                                pb = ps_mm[(m * 2 + j) % 4]
                                P.wait("tensor", lw2[slot], wld)
                                if m == 0:
                                    P.wait("tensor", ve, gh_ev[(EC - 2 + j, 1 if j == 0 else 0)])
                                    P.wait("tensor", ve, gh_ev[(EC - 1, j)])
                                elif m == 1:
                                    P.wait("tensor", ve, madd_ev[(0, j)])
                                else:
                                    P.wait("tensor", ve, madd_ev[(m - 2, j)])
                                mm_ev = None
                                for hc in range(EC):
                                    mm_ev = P.op("tensor", lambda e, m=m, hc=hc, j=j, slot=slot, pb=pb: e.matmul(
                                        out=pb[:, 0:TC],
                                        lhsT=wm[slot][:, hc * 128:(hc + 1) * 128],
                                        rhs=ghid[:, (j * EC + hc) * TC:(j * EC + hc + 1) * TC],
                                        start=(hc == 0), stop=(hc == EC - 1)), pe)
                                wm_user[slot] = mm_ev
                                P.wait("vector", pe, mm_ev)
                                P.wait("vector", lbc, gbc_ev)
                                P.op("vector", lambda e, j=j, pb=pb: e.tensor_mul(
                                    out=ft[j][:], in0=pb[:, 0:TC], in1=gbc[j][:]), ve)
                                mv = moe_sb[:, (j * KC + m) * TC:(j * KC + m + 1) * TC]
                                madd_ev[(m, j)] = P.op("vector", lambda e, mv=mv, j=j: e.tensor_add(
                                    out=mv, in0=mv, in1=ft[j][:]), ve)
                    for j, tb in enumerate(tbs):
                        t0 = tb * TC
                        emit_x_load(b, tb, x_fm, xtm, ps_tr)
                        P.wait("sync", ve)
                        for kc in range(KC):
                            dma(mix_fm[:, kc * TC:(kc + 1) * TC], s_mix[b, kc, :, t0:t0 + TC], lmix)
                        P.wait("sync", stc)
                        dma(gfm[:], s_gates[b, tb], lbc)
                        P.wait("vector", lbc)
                        P.op("vector", lambda e: e.tensor_copy(out=gfm_bf[:], in_=gfm[:]), ve)
                        P.wait("tensor", ve)
                        for m in range(KC):
                            P.op("tensor", lambda e, m=m: e.matmul(
                                out=ps_b2[:, 0:TC], lhsT=b2_bf[:, m * 128:(m + 1) * 128],
                                rhs=gfm_bf[:], start=True, stop=True), pe)
                            P.wait("vector", pe)
                            mv = moe_sb[:, (j * KC + m) * TC:(j * KC + m + 1) * TC]
                            P.op("vector", lambda e, mv=mv: e.tensor_add(
                                out=mv, in0=mv, in1=ps_b2[:, 0:TC]), ve)
                            P.wait("tensor", ve)
                            P.op("vector", lambda e, m=m: e.scalar_tensor_tensor(
                                out=x_fm[:, m * TC:(m + 1) * TC],
                                in0=x_fm[:, m * TC:(m + 1) * TC], scalar=2.0,
                                in1=mix_fm[:, m * TC:(m + 1) * TC],
                                op0=ALU.mult, op1=ALU.add), ve)
                            P.op("vector", lambda e, m=m, mv=mv: e.tensor_add(
                                out=x_fm[:, m * TC:(m + 1) * TC],
                                in0=x_fm[:, m * TC:(m + 1) * TC], in1=mv), ve)
                        emit_layernorm(x_fm, mix_fm, x2_bf, sq_bf, ps_st, ps_mm, rrow, trow)
                        P.op("vector", lambda e: e.tensor_copy(out=x2_bf[:], in_=mix_fm[:]), ve)
                        P.wait("tensor", ve)
                        for kc in range(KC):
                            P.op("tensor", lambda e, kc=kc: e.matmul(
                                out=ps_lg[:, 0:TC], lhsT=gww_bf[:, kc * 8:(kc + 1) * 8],
                                rhs=x2_bf[:, kc * TC:(kc + 1) * TC],
                                start=(kc == 0), stop=(kc == KC - 1)), pe)
                        P.wait("vector", pe)
                        P.op("vector", lambda e: e.tensor_copy(out=lg_sb[:], in_=ps_lg[0:8, 0:TC]), ve)
                        emit_logit_transpose(lg_sb, lgtm, ps_tr)
                        col = lambda tt, i: tk[:, tt * 20 + i: tt * 20 + i + 1]
                        lgm_ = lambda tt: tk[:, tt * 20 + 9: tt * 20 + 17]
                        lg8_ = lambda tt: lgtm[:, tt * 8:(tt + 1) * 8]
                        g8_ = lambda tt: gtm[:, tt * 8:(tt + 1) * 8]
                        for tt in range(NTT):
                            P.op("vector", lambda e, tt=tt: e.reduce_max(
                                out=col(tt, 0), in_=lg8_(tt), axis=AX), ve)
                        for tt in range(NTT):
                            P.op("vector", lambda e, tt=tt: e.tensor_scalar(
                                out=g8_(tt), in0=lg8_(tt), scalar1=col(tt, 0),
                                scalar2=None, op0=ALU.is_equal), ve)
                        for tt in range(NTT):
                            P.op("vector", lambda e, tt=tt: e.scalar_tensor_tensor(
                                out=lgm_(tt), in0=g8_(tt), scalar=-1e30, in1=lg8_(tt),
                                op0=ALU.mult, op1=ALU.add), ve)
                        for tt in range(NTT):
                            P.op("vector", lambda e, tt=tt: e.reduce_max(
                                out=col(tt, 1), in_=lgm_(tt), axis=AX), ve)
                        for tt in range(NTT):
                            P.op("vector", lambda e, tt=tt: e.tensor_sub(
                                out=col(tt, 2), in0=col(tt, 1), in1=col(tt, 0)), ve)
                        for tt in range(NTT):
                            P.op("vector", lambda e, tt=tt: e.tensor_copy(
                                out=col(tt, 7), in_=g8_(tt)[:, 0:1]), ve)
                        for tt in range(NTT):
                            P.op("vector", lambda e, tt=tt: e.tensor_scalar(
                                out=col(tt, 8), in0=lgm_(tt)[:, 1:2], scalar1=col(tt, 1),
                                scalar2=None, op0=ALU.is_equal), ve)
                        for tt in range(NTT):
                            nmv = P.op("vector", lambda e, tt=tt: e.tensor_scalar_mul(
                                out=col(tt, 3), in0=col(tt, 0), scalar1=-1.0), ve)
                        P.wait("scalar", ve, nmv)
                        for tt in range(NTT):
                            P.op("scalar", lambda e, tt=tt: e.activation(
                                out=lgm_(tt), in_=lg8_(tt), func=AF.Exp,
                                bias=col(tt, 3)), se)
                        for tt in range(NTT):
                            ex2 = P.op("scalar", lambda e, tt=tt: e.activation(
                                out=col(tt, 4), in_=col(tt, 2), func=AF.Exp), se)
                        P.wait("vector", se, ex2)
                        for tt in range(NTT):
                            zv = P.op("vector", lambda e, tt=tt: e.reduce_sum(
                                out=col(tt, 5), in_=lgm_(tt), axis=AX), ve)
                        P.wait("scalar", ve, zv)
                        for tt in range(NTT):
                            P.op("scalar", lambda e, tt=tt: e.activation(
                                out=col(tt, 5), in_=col(tt, 5), func=AF.Ln), se)
                        for tt in range(NTT):
                            rzv = P.op("scalar", lambda e, tt=tt: e.activation(
                                out=col(tt, 6), in_=col(tt, 5), func=AF.Exp,
                                scale=-1.0), se)
                        P.wait("vector", se, rzv)
                        for tt in range(NTT):
                            P.op("vector", lambda e, tt=tt: e.tensor_mul(
                                out=col(tt, 7), in0=col(tt, 7), in1=col(tt, 6)), ve)
                        for tt in range(NTT):
                            P.op("vector", lambda e, tt=tt: e.tensor_mul(
                                out=col(tt, 8), in0=col(tt, 8), in1=col(tt, 4)), ve)
                        for tt in range(NTT):
                            P.op("vector", lambda e, tt=tt: e.tensor_mul(
                                out=col(tt, 8), in0=col(tt, 8), in1=col(tt, 6)), ve)
                        for tt in range(NTT):
                            P.op("vector", lambda e, tt=tt: e.tensor_copy(
                                out=gtm[:, tt * 8 + 0: tt * 8 + 1], in_=col(tt, 7)), ve)
                        for tt in range(NTT):
                            P.op("vector", lambda e, tt=tt: e.tensor_copy(
                                out=gtm[:, tt * 8 + 1: tt * 8 + 2], in_=col(tt, 8)), ve)
                        P.wait("vector", stc)
                        emit_rows_back(gtm, gfm, 2, ps_tr)
                        P.wait("sync", ve)
                        store(s_coef[b, tb], gfm[0:2, :], stc)
                        P.wait("sync", stc)
                        P.wait("sync", ve)
                        dma(gbc[0][:], s_coef[b, tb, 0:1, :].partition_broadcast(128), lbc)
                        dma(gbc[1][:], s_coef[b, tb, 1:2, :].partition_broadcast(128), lbc)
                        cf_ev = lbc.n
                        wm_user2 = {0: pe.n, 1: pe.n}
                        sil_ev, cmb_ev = {}, {}
                        for ei, (sw, bsb) in enumerate(((s_w0t, fb0), (s_w1t, fb1))):
                            for m in range(KC):
                                i = ei * KC + m
                                slot = i % 2
                                P.wait("gpsimd", pe, wm_user2[slot])
                                wld = dma(wm[slot][:, 0:KC * 128].rearrange("p (k n) -> p k n", k=KC),
                                          sw[:, :, m * 128:(m + 1) * 128].rearrange("k p n -> p k n"), lw2[slot], q="gpsimd")
                                pb = ps_mm[i % 4]
                                P.wait("tensor", lw2[slot], wld)
                                if i >= 4:
                                    P.wait("tensor", se, sil_ev[i - 4])
                                    P.wait("tensor", ve, cmb_ev[i - 4])
                                mm_ev = None
                                for kc in range(KC):
                                    mm_ev = P.op("tensor", lambda e, m=m, kc=kc, slot=slot, pb=pb: e.matmul(
                                        out=pb[:, 0:TC],
                                        lhsT=wm[slot][:, kc * 128:(kc + 1) * 128],
                                        rhs=x2_bf[:, kc * TC:(kc + 1) * TC],
                                        start=(kc == 0), stop=(kc == KC - 1)), pe)
                                wm_user2[slot] = mm_ev
                                P.wait("scalar", pe, mm_ev)
                                if i >= 2:
                                    P.wait("scalar", ve, cmb_ev[i - 2])
                                et = ft[slot]
                                sil_ev[i] = P.op("scalar", lambda e, m=m, pb=pb, et=et, bsb=bsb: e.activation(
                                    out=et[:], in_=pb[:, 0:TC],
                                    func=AF.Silu, bias=bsb[:, m:m + 1]), se)
                                P.wait("vector", se, sil_ev[i])
                                if i == 0:
                                    P.wait("vector", lbc, cf_ev)
                                if ei == 0:
                                    cmb_ev[i] = P.op("vector", lambda e, m=m, et=et: e.tensor_mul(
                                        out=f_sb[:, m * TC:(m + 1) * TC], in0=et[:],
                                        in1=gbc[0][:]), ve)
                                else:
                                    P.op("vector", lambda e, m=m, et=et: e.tensor_mul(
                                        out=et[:], in0=et[:], in1=gbc[1][:]), ve)
                                    cmb_ev[i] = P.op("vector", lambda e, m=m, et=et: e.tensor_add(
                                        out=f_sb[:, m * TC:(m + 1) * TC],
                                        in0=f_sb[:, m * TC:(m + 1) * TC], in1=et[:]), ve)
                        for m in range(KC):
                            P.op("vector", lambda e, m=m: e.tensor_add(
                                out=f_sb[:, m * TC:(m + 1) * TC],
                                in0=f_sb[:, m * TC:(m + 1) * TC],
                                in1=mix_fm[:, m * TC:(m + 1) * TC]), ve)
                        emit_layernorm(f_sb, x_fm, x2_bf, sq_bf, ps_st, ps_mm, rrow, trow)
                        P.wait("tensor", ve)
                        ocp_ev = {}
                        for tt in range(NTT):
                            P.wait("vector", stc)
                            for kc in range(KC):
                                i = tt * KC + kc
                                if i >= 2:
                                    P.wait("tensor", ve, ocp_ev[i - 2])
                                tr = P.op("tensor", lambda e, tt=tt, kc=kc, i=i: e.transpose(
                                    out=ps_tr[i % 2][:],
                                    in_=x_fm[:, kc * TC + tt * 128: kc * TC + (tt + 1) * 128],
                                    identity=id_f[:]), pe)
                                P.wait("vector", pe, tr)
                                ocp_ev[i] = P.op("vector", lambda e, kc=kc, i=i: e.tensor_copy(
                                    out=xtm[1][:, kc * 128:(kc + 1) * 128], in_=ps_tr[i % len(ps_tr)]), ve)
                            P.wait("sync", ve)
                            store(out[b, t0 + tt * 128: t0 + (tt + 1) * 128, :], xtm[1][:], stc)
                P.barrier(ALLSEMS)

        for b in range(NB):
            if "A" in phases:
                phase_A(b)
            if "B" in phases:
                phase_B(b)
            if "C" in phases:
                phase_C(b)

        for sm in ALLSEMS:
            P.wait("sync", sm)

        with nc.Block() as block:
            @block.sync
            def _(eng):
                for fn in P.streams["sync"]:
                    fn(eng)

            @block.vector
            def _(eng):
                for fn in P.streams["vector"]:
                    fn(eng)

            @block.scalar
            def _(eng):
                for fn in P.streams["scalar"]:
                    fn(eng)

            @block.tensor
            def _(eng):
                for fn in P.streams["tensor"]:
                    fn(eng)

            @block.gpsimd
            def _(eng):
                for fn in P.streams["gpsimd"]:
                    fn(eng)

    return nc


def kernel(**inputs):
    ncores = 8
    import ml_dtypes
    x_full = np.ascontiguousarray(np.asarray(inputs["L0_attn_x"], dtype=np.float32))
    base = {
        "norm1_w": inputs["L0_attn_norm1_w"], "norm2_w": inputs["L0_attn_norm2_w"],
        "in_proj_w": inputs["L0_attn_in_proj_w"], "conv_w": inputs["L0_attn_conv_w"],
        "conv_b": inputs["L0_attn_conv_b"], "x_proj_w": inputs["L0_attn_x_proj_w"],
        "dt_proj_w": inputs["L0_attn_dt_proj_w"], "dt_proj_b": inputs["L0_attn_dt_proj_b"],
        "A_log": inputs["L0_attn_A_log"], "D_param": inputs["L0_attn_D_param"],
        "out_proj_w": inputs["L0_attn_out_proj_w"], "router_w": inputs["L0_attn_router_w"],
        "w1": inputs["L0_attn_w1"], "b1": inputs["L0_attn_b1"],
        "w2": inputs["L0_attn_w2"], "b2": inputs["L0_attn_b2"],
        "gate_w": inputs["L0_ffn_gate_w"], "ffn_w0": inputs["L0_ffn_w0"],
        "ffn_b0": inputs["L0_ffn_b0"], "ffn_w1": inputs["L0_ffn_w1"],
        "ffn_b1": inputs["L0_ffn_b1"],
        "ident_bf": np.eye(128, dtype=ml_dtypes.bfloat16),
        "ident_f32": np.eye(128, dtype=np.float32),
    }
    base = {k: np.ascontiguousarray(np.asarray(v, dtype=(ml_dtypes.bfloat16 if k == "ident_bf" else np.float32)))
            for k, v in base.items()}
    nc = build_program()
    in_maps = []
    for c in range(ncores):
        m = dict(base)
        m["x"] = np.ascontiguousarray(x_full[c * NB:(c + 1) * NB])
        in_maps.append(m)
    res = run_bass_kernel_spmd(nc, in_maps, core_ids=list(range(ncores)))
    return np.concatenate([np.asarray(r["out"], np.float32) for r in res.results], axis=0)
